# version 14
# baseline (speedup 1.0000x reference)
import numpy as np

import concourse.bass as bass
import concourse.mybir as mybir
from concourse.bass_utils import run_bass_kernel_spmd

F32 = mybir.dt.float32
BF16 = mybir.dt.bfloat16
AF = mybir.ActivationFunctionType
ALU = mybir.AluOpType
AX = mybir.AxisListType

D = 1024
H = 16
DT = 8
NCORES = 8
EPS = 1e-5
ISCALE = 1.0 / 32.0


class Sem:
    def __init__(self, nc, name):
        self.h = nc.semaphore(name).__enter__()
        self.n = 0

    def inc(self, instr, k=1):
        instr.then_inc(self.h, k)
        self.n += k
        return self.n


def chunks(total, size):
    out, off = [], 0
    while off < total:
        out.append((off, min(size, total - off)))
        off += size
    return out


def build_nc(b_loc=2, t_len=2048):
    TT = t_len // 128
    KT = t_len // 128
    QH = min(1024, t_len)
    NH = t_len // QH

    nc = bass.Bass("TRN2", target_bir_lowering=False)

    x_d = nc.dram_tensor("x", [b_loc * t_len, D], F32, kind="ExternalInput")
    wqT_d = nc.dram_tensor("wqT", [D, D], F32, kind="ExternalInput")
    wkT_d = nc.dram_tensor("wkT", [D, D], F32, kind="ExternalInput")
    wvT_d = nc.dram_tensor("wvT", [D, D], F32, kind="ExternalInput")
    wr_d = nc.dram_tensor("wr", [D, D], F32, kind="ExternalInput")
    wi_d = nc.dram_tensor("wi", [D, D], F32, kind="ExternalInput")
    bias_d = nc.dram_tensor("bias", [1, D], F32, kind="ExternalInput")
    cc_d = nc.dram_tensor("cc", [D, D], F32, kind="ExternalInput")
    ssn_d = nc.dram_tensor("ssn", [D, D], F32, kind="ExternalInput")
    id_d = nc.dram_tensor("idm", [128, 128], F32, kind="ExternalInput")
    e65_d = nc.dram_tensor("e65", [128, 65], F32, kind="ExternalInput")
    on1_d = nc.dram_tensor("on1", [1, 128], F32, kind="ExternalInput")
    out_d = nc.dram_tensor("out", [b_loc * t_len, D], F32, kind="ExternalOutput")
    dnm_d = nc.dram_tensor("dnm", [H, t_len], BF16, kind="Internal")
    m_d = nc.dram_tensor("m_hbm", [128, DT * D], BF16, kind="Internal")

    def wtiles(d):
        return d[:, :].rearrange("(t p) f -> p t f", p=128)

    ARENA = 202 * 1024
    _arena = nc.alloc_sbuf_tensor("arena", [128, ARENA], mybir.dt.uint8)
    _ab = nc.lookup_mloc(_arena).addr
    base = [_ab]
    amap = {}

    def A(name, shape, dtype, at=None):
        size = int(np.prod(shape[1:])) * (2 if dtype == BF16 else 4)
        size = (size + 31) // 32 * 32
        off = base[0] if at is None else at
        if at is None:
            base[0] += size
        amap[name] = off
        return nc.alloc_sbuf_tensor_at(name, list(shape), dtype, offset=off)

    r1_pad = max(DT * t_len * 2, 2 * DT * D * 2)
    n1T = A("n1T", [128, DT * t_len], BF16)
    if r1_pad > DT * t_len * 2:
        base[0] += r1_pad - DT * t_len * 2
    attnT = A("attnT", [128, DT * t_len], BF16)
    if r1_pad > DT * t_len * 2:
        base[0] += r1_pad - DT * t_len * 2
    vaug = A("vaug", [128, KT * (H * 65)], BF16)
    w_bf = A("w_bf", [128, 2 * DT * D], BF16)
    qT_b = A("qT_b", [128, 2 * t_len], BF16)
    kT_b = A("kT_b", [128, 2 * t_len], BF16)
    pt_sb = A("pt_sb", [128, 2 * QH], BF16)
    if 2 * QH * 2 < 4096:
        base[0] += 4096 - 2 * QH * 2
    qsq = A("qsq", [128, t_len], BF16)
    if t_len * 2 < 4096:
        base[0] += 4096 - t_len * 2
    x_t = A("x_t", [128, 2 * D], F32)
    id_bf = A("id_bf", [128, 128], BF16)
    n1_t = A("n1_t", [128, 2 * D], BF16)
    a_tok = A("a_tok", [128, D], F32)
    tmp_t = A("tmp_t", [128, D], F32)
    x1_t = A("x1_t", [128, D], F32)
    x2_t = A("x2_t", [128, 2 * D], F32)
    n2T_t = A("n2T_t", [128, D], BF16)
    m_stage = A("m_stage", [128, 512], BF16)
    recl = A("recl", [128, TT * 16], BF16)
    recl_f = A("recl_f", [128, TT * 16], F32)
    rex = A("rex", [128, D], F32)
    e65_bf = A("e65_bf", [128, 65], BF16)
    on1_f = A("on1_f", [1, 128], F32)
    on64 = A("on64", [65, 128], BF16)
    negb = A("negb", [128, H], F32)
    nb_bf = A("nb_bf", [65, 1], BF16)
    stat = A("stat", [128, 48], F32)
    st2 = A("st2", [65, 12], F32)
    eps_t = A("eps_t", [128, 1], F32)
    fence_t = A("fence_t", [128, 1], F32)
    sums_x = A("sums_x", [128, 16], F32)
    m_ld = nc.alloc_sbuf_tensor_at("m_ld", [128, DT * D], BF16,
                                   offset=amap["n1T"])
    p1T = nc.alloc_sbuf_tensor_at("p1T", [128, DT * D], BF16,
                                  offset=amap["n1T"])
    p2T = nc.alloc_sbuf_tensor_at("p2T", [128, DT * D], BF16,
                                  offset=amap["n1T"] + DT * D * 2)
    cc_bf = nc.alloc_sbuf_tensor_at("cc_bf", [128, DT * D], BF16,
                                    offset=amap["attnT"])
    ss_bf = nc.alloc_sbuf_tensor_at("ss_bf", [128, DT * D], BF16,
                                    offset=amap["attnT"] + DT * D * 2)
    stage = nc.alloc_sbuf_tensor_at("stage", [1, t_len], F32,
                                    offset=amap["x_t"])
    bias_s = nc.alloc_sbuf_tensor_at("bias_s", [1, D], F32, offset=amap["rex"])
    x1b1 = nc.alloc_sbuf_tensor_at("x1b1", [128, D], F32, offset=amap["rex"])
    x1b2 = nc.alloc_sbuf_tensor_at("x1b2", [128, D], F32, offset=amap["w_bf"])
    n2Tb = nc.alloc_sbuf_tensor_at("n2Tb", [128, D], BF16,
                                   offset=amap["w_bf"] + 4096)
    junk = nc.alloc_sbuf_tensor_at("junk", [128, D], BF16,
                                   offset=amap["w_bf"] + 6144)
    y_t0 = nc.alloc_sbuf_tensor_at("y_t0", [128, D], F32, offset=amap["pt_sb"])
    y_t1 = nc.alloc_sbuf_tensor_at("y_t1", [128, D], F32, offset=amap["qsq"])
    bias_bf = A("bias_bf", [1, D], BF16)
    assert base[0] <= _ab + ARENA, f"SBUF over: {base[0]-_ab}"

    pT = Sem(nc, "pT")
    pV = Sem(nc, "pV")
    pS = Sem(nc, "pS")
    dxs = [Sem(nc, "dx0"), Sem(nc, "dx1")]
    dsm = Sem(nc, "dsm")
    dwA = Sem(nc, "dwA")
    dwB = Sem(nc, "dwB")
    dwc = Sem(nc, "dwc")
    dwm = Sem(nc, "dwm")
    dms = Sem(nc, "dms")
    dd = Sem(nc, "dd")
    dtp = Sem(nc, "dtp")
    dos = [Sem(nc, "do0"), Sem(nc, "do1")]

    TE, VE, SE, SY, GP = nc.tensor, nc.vector, nc.scalar, nc.sync, nc.gpsimd

    def V(fn, *waits):
        for (s, v) in waits:
            if v:
                VE.wait_ge(s.h, v)
        VE.wait_ge(pV.h, pV.n)
        return pV.inc(fn())

    def S(fn, *waits):
        for (s, v) in waits:
            if v:
                SE.wait_ge(s.h, v)
        SE.wait_ge(pS.h, pS.n)
        return pS.inc(fn())

    def TW(*waits):
        for (s, v) in waits:
            if v:
                TE.wait_ge(s.h, v)

    xg = {}

    def g_wait_T(key):
        s, v = xg.get(key, (pT, 0))
        if v:
            TE.wait_ge(s.h, v)

    def ln_begin(src_ap, cb, sum_on="S", src_w=None):
        w = [src_w] if src_w else []
        if sum_on == "V":
            V(lambda: VE.reduce_sum(stat[:, cb:cb + 1], src_ap, axis=AX.X),
              *w)
        else:
            S(lambda: SE.activation(junk[:, :], src_ap, AF.Identity, bias=0.0,
                                    scale=1.0, accum_out=stat[:, cb:cb + 1]),
              *w)
        return S(lambda: SE.activation(junk[:, :], src_ap, AF.Square,
                                       bias=0.0, scale=1.0,
                                       accum_out=stat[:, cb + 1:cb + 2]), *w)

    def ln_mid(cb, vb, sum_ap=None):
        sap = stat[:, cb:cb + 1] if sum_ap is None else sum_ap
        V(lambda: VE.tensor_scalar_mul(stat[:, cb + 2:cb + 3],
                                       sap, 1.0 / D), (pS, vb))
        V(lambda: VE.tensor_mul(stat[:, cb + 3:cb + 4], stat[:, cb + 2:cb + 3],
                                stat[:, cb + 2:cb + 3]))
        return V(lambda: VE.tensor_scalar(stat[:, cb + 4:cb + 5],
                                          stat[:, cb + 1:cb + 2], 1.0 / D,
                                          stat[:, cb + 3:cb + 4],
                                          op0=ALU.mult, op1=ALU.subtract))

    def ln_rstd(cb, vmid):
        S(lambda: SE.activation(stat[:, cb + 5:cb + 6], stat[:, cb + 4:cb + 5],
                                AF.Ln, bias=eps_t[:, :], scale=1.0),
          (pV, vmid))
        return S(lambda: SE.activation(stat[:, cb + 5:cb + 6],
                                       stat[:, cb + 5:cb + 6], AF.Exp,
                                       bias=0.0, scale=-0.5))

    def ln_stats(src_ap, cb, mode="VS", src_w=None):
        vb = ln_begin(src_ap, cb, sum_on=mode[0], src_w=src_w)
        return ln_rstd(cb, ln_mid(cb, vb))

    def ln_apply(src_ap, out_ap, cb, vs, *waits):
        return V(lambda: VE.tensor_scalar(out_ap, src_ap,
                                          stat[:, cb + 2:cb + 3],
                                          stat[:, cb + 5:cb + 6],
                                          op0=ALU.subtract, op1=ALU.mult),
                 (pS, vs), *waits)

    def ln_block(src_ap, out_ap, residual_ap):
        V(lambda: VE.reduce_sum(stat[:, 0:1], src_ap, axis=AX.X))
        V(lambda: VE.scalar_tensor_tensor(tmp_t[:, :], src_ap, 1.0, src_ap,
                                          op0=ALU.mult, op1=ALU.mult,
                                          accum_out=stat[:, 1:2]))
        V(lambda: VE.tensor_scalar_mul(stat[:, 2:3], stat[:, 0:1], 1.0 / D))
        V(lambda: VE.tensor_mul(stat[:, 3:4], stat[:, 2:3], stat[:, 2:3]))
        vv = V(lambda: VE.tensor_scalar(stat[:, 4:5], stat[:, 1:2], 1.0 / D,
                                        stat[:, 3:4], op0=ALU.mult,
                                        op1=ALU.subtract))
        S(lambda: SE.activation(stat[:, 5:6], stat[:, 4:5], AF.Ln,
                                bias=eps_t[:, :], scale=1.0), (pV, vv))
        vs = S(lambda: SE.activation(stat[:, 5:6], stat[:, 5:6], AF.Exp,
                                     bias=0.0, scale=-0.5))
        if residual_ap is None:
            return V(lambda: VE.tensor_scalar(out_ap, src_ap, stat[:, 2:3],
                                              stat[:, 5:6], op0=ALU.subtract,
                                              op1=ALU.mult), (pS, vs))
        V(lambda: VE.tensor_scalar(tmp_t[:, :], src_ap, stat[:, 2:3],
                                   stat[:, 5:6], op0=ALU.subtract,
                                   op1=ALU.mult), (pS, vs))
        return V(lambda: VE.tensor_add(out_ap, tmp_t[:, :], residual_ap))

    with (
        nc.psum_tensor("pv_ps", [128, 1024], F32) as pv_ps,
        nc.psum_tensor("pj_ps", [128, 1024], F32) as pj_ps,
        nc.psum_tensor("s0_ps", [128, 1024], F32) as s0_ps,
    ):
        pjg = [(pV, 0), (pV, 0)]
        pj_par = [0]

        def pj_chain(emit_mms, evac, *mm_waits):
            w = pj_par[0]
            s, v = pjg[w]
            if v:
                TE.wait_ge(s.h, v)
            TW(*mm_waits)
            last = emit_mms(pj_ps[:, w * 512:(w + 1) * 512])
            pT.inc(last)
            g = evac(pj_ps[:, w * 512:(w + 1) * 512], pT.n)
            if not isinstance(g, tuple):
                g = (pV, g)
            pjg[w] = g
            pj_par[0] ^= 1
            return g[1]

        dwc.inc(GP.dma_start(id_bf[:, :], id_d[:, :]), 16)
        dwc.inc(GP.dma_start(e65_bf[:, :], e65_d[:, :]), 16)
        dwc.inc(GP.dma_start(on64[0:1, :], on1_d[:, :]), 16)
        dwc.inc(GP.dma_start(on64[64:65, :], on1_d[:, :]), 16)
        for t in range(DT):
            dwc.inc(GP.dma_start(cc_bf[:, t * D:(t + 1) * D],
                                 wtiles(cc_d)[:, t, :]), 16)
        for t in range(DT):
            dwc.inc(GP.dma_start(ss_bf[:, t * D:(t + 1) * D],
                                 wtiles(ssn_d)[:, t, :]), 16)
        for t in range(DT):
            dwA.inc(GP.dma_start(w_bf[:, t * D:(t + 1) * D],
                                 wtiles(wr_d)[:, t, :]), 16)
        for t in range(DT):
            dwB.inc(GP.dma_start(w_bf[:, (DT + t) * D:(DT + t + 1) * D],
                                 wtiles(wi_d)[:, t, :]), 16)
        v_setup_c, v_setup_a, v_setup_b = dwc.n, dwA.n, dwB.n
        dsm.inc(SY.dma_start(bias_s[:, :], bias_d[:, :]), 16)
        dsm.inc(SY.dma_start(on1_f[:, :], on1_d[:, :]), 16)
        v_small = dsm.n

        V(lambda: VE.memset(eps_t[:, :], EPS))
        va = vaug[:, :].rearrange("p (k c) -> p k c", c=65)
        V(lambda: VE.memset(va[:, :, 64:65], 1.0))

        V(lambda: VE.tensor_copy(bias_bf[:, :], bias_s[:, :]),
          (dsm, v_small))

        TW((dwc, v_setup_c), (dwA, v_setup_a), (dwB, v_setup_b))

        def emit_mat(dst, pairs, spill=False):
            for i in range(DT):
                for oc in range(2):
                    def mk(ps, i=i, oc=oc):
                        first = True
                        np_ = len(pairs)
                        for pi, (lw, lblk, rt) in enumerate(pairs):
                            for dt in range(DT):
                                mm = TE.matmul(
                                    ps,
                                    lw[:, lblk + dt * D + i * 128:
                                       lblk + dt * D + i * 128 + 128],
                                    rt[:, dt * D + oc * 512:
                                       dt * D + oc * 512 + 512],
                                    start=first,
                                    stop=(pi == np_ - 1 and dt == DT - 1))
                                first = False
                        return mm

                    if not spill:
                        def ev(ps, tv, i=i, oc=oc):
                            return V(lambda: VE.tensor_copy(
                                dst[:, i * D + oc * 512:i * D + oc * 512 + 512],
                                ps), (pT, tv))
                    else:
                        def ev(ps, tv, i=i, oc=oc):
                            vv = V(lambda: VE.tensor_scalar_mul(
                                m_stage[:, :], ps, 1.0 / D), (pT, tv),
                                xg.get("m_stage", (pV, 0)))
                            SY.wait_ge(pV.h, vv)
                            st = SY.dma_start(
                                m_d[:, (i * 2 + oc) * 512:
                                    (i * 2 + oc + 1) * 512], m_stage[:, :])
                            dms.inc(st, 16)
                            xg["m_stage"] = (dms, dms.n)
                            return vv
                    pj_chain(mk, ev)

        emit_mat(p1T, [(w_bf, 0, cc_bf), (w_bf, DT * D, ss_bf)])
        v_sneg = V(lambda: VE.tensor_scalar_mul(ss_bf[:, :], ss_bf[:, :], -1.0),
                   (pT, pT.n))
        TW((pV, v_sneg))
        emit_mat(p2T, [(w_bf, DT * D, cc_bf), (w_bf, 0, ss_bf)])
        TW((pV, pV.n))
        emit_mat(None, [(p1T, 0, cc_bf), (p2T, 0, ss_bf)], spill=True)
        v_m_T = pT.n
        v_m_spill = dms.n
        xg["n1T_w"] = (pT, v_m_T)
        xg["attnT_w"] = (pT, v_m_T)
        xg["slotA"] = (pT, v_m_T)
        xg["slotA2"] = (pT, v_m_T)

        GP.wait_ge(pT.h, v_m_T)
        for t in range(DT):
            dwB.inc(GP.dma_start(w_bf[:, (DT + t) * D:(DT + t + 1) * D],
                                 wtiles(wkT_d)[:, t, :]), 16)
        v_wk = dwB.n

        for b in range(b_loc):
            with nc.psum_tensor(f"tb1_{b}", [128, 2048], BF16) as tb:
                g_wait_T("psum_b67")
                V(lambda: VE.memset(fence_t[0:1, 0:1], 0.0), (pT, pT.n))
                tbg = [(pV, 0), (pV, 0)]
                first_evac = [True]
                vs1 = {}
                xv1 = {}
                g_tp1 = {}

                def b1_F(j):
                    sl = j % 2
                    sv, vv = xg.get(f"x{sl}", (pV, 0))
                    if vv:
                        SY.wait_ge(sv.h, vv)
                    sv, vv = xg.get("stage_rd", (dd, 0))
                    if vv and j < 2:
                        SY.wait_ge(sv.h, vv)
                    ld = SY.dma_start(x_t[:, sl * D:(sl + 1) * D],
                                      x_d[b * t_len + j * 128:
                                          b * t_len + j * 128 + 128, :])
                    dxs[sl].inc(ld, 16)
                    xv1[j] = dxs[sl].n
                    V(lambda sl=sl, j=j: VE.reduce_sum(
                        sums_x[:, j:j + 1], x_t[:, sl * D:(sl + 1) * D],
                        axis=AX.X), (dxs[sl], xv1[j]))
                    vs1[j] = S(lambda sl=sl: SE.activation(
                        junk[:, :], x_t[:, sl * D:(sl + 1) * D], AF.Square,
                        bias=0.0, scale=1.0,
                        accum_out=stat[:, 8 * sl + 1:8 * sl + 2]),
                        (dxs[sl], xv1[j]))

                def b1_G(j):
                    sl = j % 2
                    vr1 = ln_rstd(8 * sl, ln_mid(8 * sl, vs1[j],
                                                 sums_x[:, j:j + 1]))
                    v_ap = ln_apply(x_t[:, sl * D:(sl + 1) * D],
                                    n1_t[:, sl * D:(sl + 1) * D], 8 * sl,
                                    vr1, (pT, g_tp1.get(j - 2, 0)))
                    xg[f"x{sl}"] = (pV, v_ap)
                    sv, vv = tbg[sl]
                    if vv:
                        TE.wait_ge(sv.h, vv)
                    TW((pV, v_ap))
                    for dt in range(DT):
                        tp = TE.transpose(
                            tb[:, sl * 1024 + dt * 128:sl * 1024 + dt * 128 + 128],
                            n1_t[:, sl * D + dt * 128:sl * D + dt * 128 + 128],
                            id_bf[:, :])
                    pT.inc(tp)
                    g_tp1[j] = pT.n
                    dst = n1T[:, :].rearrange("p (t q) -> p t q", q=t_len)[
                        :, :, j * 128:j * 128 + 128]
                    src = tb[:, sl * 1024:sl * 1024 + 1024].rearrange(
                        "p (t q) -> p t q", q=128)
                    ex = (xg["n1T_w"],) if first_evac[0] else ()
                    first_evac[0] = False
                    vv = V(lambda dst=dst, src=src: VE.tensor_copy(dst, src),
                           (pT, pT.n), *ex)
                    tbg[sl] = (pV, vv)

                for r in range(TT + 1):
                    if r < TT:
                        b1_F(r)
                    if r - 1 >= 0:
                        b1_G(r - 1)
                v_b1 = pV.n
                xg["psum_b67"] = (pV, pV.n)
                xg["slotA"] = (pV, pV.n)

            with nc.psum_tensor(f"s1_{b}", [128, 1024], F32) as s1_ps:
                g_wait_T("psum_b67")
                g = xg["slotA"]
                GP.wait_ge(g[0].h, g[1])
                g = xg.get("slotA2", (pT, 0))
                if g[1]:
                    GP.wait_ge(g[0].h, g[1])
                for t in range(DT):
                    dwA.inc(GP.dma_start(w_bf[:, t * D:(t + 1) * D],
                                         wtiles(wvT_d)[:, t, :]), 16)
                v_wv = dwA.n
                vaug_g = xg.get("vaug_w", (pT, 0))
                first_vevac = [True]
                for j in range(TT):
                    for oc in range(2):
                        def mk(ps, j=j, oc=oc):
                            for dt in range(DT):
                                mm = TE.matmul(
                                    ps,
                                    n1T[:, dt * t_len + j * 128:
                                        dt * t_len + j * 128 + 128],
                                    w_bf[:, dt * D + oc * 512:
                                         dt * D + oc * 512 + 512],
                                    start=(dt == 0), stop=(dt == DT - 1))
                            return mm

                        def ev(ps, tv, j=j, oc=oc):
                            dstv = vaug[:, j * (H * 65):
                                        (j + 1) * (H * 65)].rearrange(
                                "p (h c) -> p h c", c=65)[
                                :, oc * 8:(oc + 1) * 8, 0:64]
                            srcv = ps.rearrange("p (h c) -> p h c", c=64)
                            ex = (vaug_g,) if first_vevac[0] else ()
                            first_vevac[0] = False
                            return V(lambda: VE.tensor_copy(dstv, srcv),
                                     (pT, tv), *ex)
                        w0 = (dwA, v_wv) if (j == 0 and oc == 0) else (pV, 0)
                        w1 = (pV, v_b1) if (j == 0 and oc == 0) else (pV, 0)
                        pj_chain(mk, ev, w0, w1)
                GP.wait_ge(pT.h, pT.n)
                for t in range(DT):
                    dwA.inc(GP.dma_start(w_bf[:, t * D:(t + 1) * D],
                                         wtiles(wqT_d)[:, t, :]), 16)
                xg["v_wq"] = dwA.n

                def bg_pair(p):
                    par = p % 2

                    def mkproj_a(wbase, tc, ln):
                        def go():
                            w = pj_par[0]
                            sg_, vg_ = pjg[w]
                            if vg_:
                                TE.wait_ge(sg_.h, vg_)
                            w0 = (dwA, xg["v_wq"]) if wbase == 0 else \
                                (dwB, v_wk)
                            TW(w0)
                            for dt in range(DT // 2):
                                TE.matmul(
                                    pj_ps[:, w * 512:w * 512 + ln],
                                    w_bf[:, wbase + dt * D + p * 128:
                                         wbase + dt * D + p * 128 + 128],
                                    n1T[:, dt * t_len + tc:
                                        dt * t_len + tc + ln],
                                    start=(dt == 0), stop=False)
                        return go

                    def mkproj_b(wbase, tc, ln):
                        def go():
                            w = pj_par[0]
                            qkg = xg.get(f"qk{par}", (pT, 0))
                            for dt in range(DT // 2, DT):
                                mm = TE.matmul(
                                    pj_ps[:, w * 512:w * 512 + ln],
                                    w_bf[:, wbase + dt * D + p * 128:
                                         wbase + dt * D + p * 128 + 128],
                                    n1T[:, dt * t_len + tc:
                                        dt * t_len + tc + ln],
                                    start=False, stop=(dt == DT - 1))
                            pT.inc(mm)
                            dstT = qT_b if wbase == 0 else kT_b
                            vv = V(lambda: VE.tensor_copy(
                                dstT[:, par * t_len + tc:
                                     par * t_len + tc + ln],
                                pj_ps[:, w * 512:w * 512 + ln]),
                                (pT, pT.n), qkg)
                            pjg[w] = (pV, vv)
                            pj_par[0] ^= 1
                        return go

                    for wbase in (0, DT * D):
                        for tc, ln in chunks(t_len, 512):
                            yield (4, mkproj_a(wbase, tc, ln))
                            yield (4, mkproj_b(wbase, tc, ln))

                    yield (0, lambda: xg.__setitem__("v_qkev", pV.n))

                    def mk_qsq(srcT):
                        def go():
                            xg["v_qsq"] = V(
                                lambda: VE.scalar_tensor_tensor(
                                    qsq[:, :],
                                    srcT[:, par * t_len:(par + 1) * t_len],
                                    1.0,
                                    srcT[:, par * t_len:(par + 1) * t_len],
                                    op0=ALU.mult, op1=ALU.mult),
                                (pV, xg["v_qkev"]), xg.get("qsq_rd", (pT, 0)),
                                xg.get("pt_w", (pV, 0)))
                        return go

                    def mk_norm(tc, ln, col):
                        def go():
                            def mk(ps):
                                return TE.matmul(ps[0:65, 0:ln], e65_bf[:, :],
                                                 qsq[:, tc:tc + ln],
                                                 start=True, stop=True)

                            def ev(ps, tv):
                                vv = V(lambda: VE.reduce_max(
                                    st2[:, col:col + 1], ps[0:65, 0:ln],
                                    axis=AX.X), (pT, tv))
                                xg["qsq_rd"] = (pT, tv)
                                return vv
                            pj_chain(mk, ev, (pV, xg["v_qsq"]))
                        return go

                    yield (0, mk_qsq(qT_b))
                    for ci, (tc, ln) in enumerate(chunks(t_len, 512)):
                        yield (1, mk_norm(tc, ln, ci))
                    yield (0, mk_qsq(kT_b))
                    for ci, (tc, ln) in enumerate(chunks(t_len, 512)):
                        yield (1, mk_norm(tc, ln, 4 + ci))

                    def mk_fin():
                        def go():
                            nch = len(chunks(t_len, 512))
                            V(lambda: VE.tensor_reduce(
                                st2[:, 8:9], st2[:, 0:nch], axis=AX.X,
                                op=ALU.max))
                            V(lambda: VE.tensor_reduce(
                                st2[:, 9:10], st2[:, 4:4 + nch], axis=AX.X,
                                op=ALU.max))
                            vv = V(lambda: VE.tensor_mul(
                                st2[:, 10:11], st2[:, 8:9], st2[:, 9:10]))
                            S(lambda: SE.activation(
                                st2[:, 11:12], st2[:, 10:11], AF.Ln,
                                bias=eps_t[0:65, :], scale=1.0), (pV, vv))
                            vs = S(lambda: SE.activation(
                                st2[:, 11:12], st2[:, 11:12], AF.Exp,
                                bias=0.0, scale=0.5))
                            vn = V(lambda: VE.tensor_scalar_mul(
                                nb_bf[:, :], st2[:, 11:12], -1.0 / 32.0),
                                (pS, vs))

                            def mk(ps):
                                for h2 in range(2):
                                    mm = TE.matmul(
                                        ps[0:128, h2:h2 + 1],
                                        on64[64 * h2:64 * h2 + 1, :],
                                        nb_bf[64 * h2:64 * h2 + 1, :],
                                        start=True, stop=True)
                                return mm

                            def ev(ps, tv):
                                vv = V(lambda: VE.tensor_copy(
                                    negb[:, 2 * p:2 * p + 2],
                                    ps[0:128, 0:2]), (pT, tv))
                                xg[f"negb{p}"] = (pV, vv)
                                return vv
                            pj_chain(mk, ev, (pV, vn))
                        return go
                    yield (1, mk_fin())

                for _, go in bg_pair(0):
                    go()

                for p in range(H // 2):
                    par = p % 2
                    bg = list(bg_pair(p + 1)) if p + 1 < H // 2 else []
                    bgi = [0]
                    n_slots = 2 * NH * KT
                    bg_total = sum(c for c, _ in bg) or 1
                    bg_done = [0]

                    def bg_step():
                        while (bg and bgi[0] < len(bg)
                               and bg_done[0] / bg_total <
                               (slot[0] + 1) / n_slots):
                            c, go = bg[bgi[0]]
                            go()
                            bg_done[0] += c
                            bgi[0] += 1

                    v_qk_here = xg.get("v_qkev", pV.n)
                    slot = [0]
                    for h2 in range(2):
                        head = 2 * p + h2
                        hb = h2 * 64
                        ptg = xg.setdefault("ptg", [(pT, 0), (pT, 0)])
                        sg = [xg.get("psum_s0", (pS, 0)),
                              xg.get("psum_s0", (pS, 0))]

                        def emit_S(kt, half, first):
                            sp = s0_ps
                            q0 = half * QH
                            s, v = sg[0]
                            if v:
                                TE.wait_ge(s.h, v)
                            if first:
                                TW((pV, v_qk_here))
                            for qc, ln in chunks(QH, 512):
                                smm = TE.matmul(
                                    sp[:, qc:qc + ln],
                                    kT_b[hb:hb + 64,
                                         par * t_len + kt * 128:
                                         par * t_len + kt * 128 + 128],
                                    qT_b[hb:hb + 64,
                                         par * t_len + q0 + qc:
                                         par * t_len + q0 + qc + ln],
                                    start=True, stop=True)
                            return pT.inc(smm)

                        for half in range(NH):
                            q0 = half * QH
                            v_smm_next = emit_S(0, half,
                                                half == 0 and h2 == 0)
                            for kt in range(KT):
                                sp = s0_ps
                                v_smm = v_smm_next
                                exw = [(pT, v_smm), ptg[kt % 2]]
                                if kt == 0 and half == 0:
                                    exw.append(xg.get(f"negb{p}", (pV, 0)))
                                    if h2 == 0 and p == 0:
                                        exw.append(xg.get("pt_w", (pV, 0)))
                                vexp = S(lambda kt=kt, sp=sp, head=head:
                                         SE.activation(
                                             pt_sb[:, (kt % 2) * QH:
                                                   (kt % 2) * QH + QH],
                                             sp[:, 0:QH], AF.Exp,
                                             bias=negb[:, head:head + 1],
                                             scale=ISCALE), *exw)
                                sg[0] = (pS, vexp)
                                sg[1] = (pS, vexp)
                                if kt + 1 < KT:
                                    v_smm_next = emit_S(kt + 1, half, False)
                                bg_step()
                                slot[0] += 1
                                TW((pS, vexp))
                                if kt == 0:
                                    g_wait_T("psum_pv")
                                for qc, ln in chunks(QH, 512):
                                    pmm = TE.matmul(
                                        pv_ps[0:65, qc:qc + ln],
                                        vaug[:, kt * (H * 65) + head * 65:
                                             kt * (H * 65) + head * 65 + 65],
                                        pt_sb[:, (kt % 2) * QH + qc:
                                              (kt % 2) * QH + qc + ln],
                                        start=(kt == 0), stop=(kt == KT - 1))
                                pT.inc(pmm)
                                ptg[kt % 2] = (pT, pT.n)
                            dte = head // 2
                            aw = [(pT, pT.n)]
                            if b == 0 and head == 0 and half == 0:
                                aw.append(xg["attnT_w"])
                            S(lambda dte=dte, hb=hb, q0=q0: SE.activation(
                                attnT[hb:hb + 64,
                                      dte * t_len + q0:dte * t_len + q0 + QH],
                                pv_ps[0:64, 0:QH], AF.Identity, bias=0.0,
                                scale=1.0), *aw)
                            sw = [xg.get("stage_rd", (dd, 0))]
                            if head == 0 and half == 0:
                                sw += [xg.get("x0", (pV, 0)),
                                       xg.get("x1", (pV, 0))]
                            v_dn = V(lambda q0=q0: VE.tensor_copy(
                                stage[:, q0:q0 + QH], pv_ps[64:65, 0:QH]),
                                (pT, pT.n), *sw)
                            xg["psum_pv"] = (pV, v_dn)
                        xg["psum_s0"] = sg[0]
                        xg["psum_s1x"] = sg[1]
                        GP.wait_ge(pV.h, v_dn)
                        if head == 0:
                            s, v = xg.get("dnm_rd", (dtp, 0))
                            if v:
                                GP.wait_ge(s.h, v)
                        dn = GP.dma_start(dnm_d[head:head + 1, :], stage[:, :])
                        dd.inc(dn, 16)
                        xg["stage_rd"] = (dd, dd.n)
                    while bgi[0] < len(bg):
                        bg_step()
                    xg[f"qk{par}"] = (pT, pT.n)
                v_attn_T = pT.n
                v_attn_S = pS.n
                xg["vaug_w"] = (pT, pT.n)
                xg["psum_b67"] = (pV, pV.n)
                xg["psum_s1"] = xg.pop("psum_s1x")

            GP.wait_ge(pT.h, v_attn_T)
            GP.wait_ge(dms.h, v_m_spill)
            for i in range(4):
                dwm.inc(GP.dma_start(m_ld[:, i * 2048:(i + 1) * 2048],
                                     m_d[:, i * 2048:(i + 1) * 2048]), 16)
            v_mld = dwm.n
            SY.wait_ge(dd.h, dd.n)
            s, v = xg.get("recl_rd", (pV, 0))
            if v:
                SY.wait_ge(s.h, v)
            for j in range(TT):
                dtp.inc(SY.dma_start(recl[:, j * 16:(j + 1) * 16],
                                     dnm_d[0:16, j * 128:(j + 1) * 128],
                                     transpose=True), 16)
            xg["dnm_rd"] = (dtp, dtp.n)
            v_recl = V(lambda: VE.reciprocal(recl_f[:, :], recl[:, :]),
                       (dtp, dtp.n))
            xg["recl_rd"] = (pV, v_recl)

            with nc.psum_tensor(f"tb4_{b}", [128, 2048], BF16) as tb:
                g_wait_T("psum_b67")
                V(lambda: VE.memset(fence_t[0:1, 0:1], 0.0), (pT, v_attn_T))
                x1s = [x1_t, x1b1, x1b2]
                ats = [a_tok, tmp_t]
                yts = [y_t0, y_t1]
                n2s = [n2T_t, n2Tb]
                vs2, vs3, vs4 = {}, {}, {}
                xv = {}
                g_div = {}
                g_tpN = {}
                g_yT = {}
                g_evN = {}

                def b4_F(j):
                    sl = j % 2
                    sv, vv = xg.get(f"x{sl}", (pV, 0))
                    if vv:
                        SY.wait_ge(sv.h, vv)
                    ld = SY.dma_start(x_t[:, sl * D:(sl + 1) * D],
                                      x_d[b * t_len + j * 128:
                                          b * t_len + j * 128 + 128, :])
                    dxs[sl].inc(ld, 16)
                    xv[j] = dxs[sl].n
                    vv = g_div.get(j - 1, 0)
                    if vv:
                        TE.wait_ge(pV.h, vv)
                    if j == 0:
                        TW((pS, v_attn_S))
                    for dt in range(DT):
                        tp = TE.transpose(
                            tb[:, dt * 128:dt * 128 + 128],
                            attnT[:, dt * t_len + j * 128:
                                  dt * t_len + j * 128 + 128],
                            id_bf[:, :])
                    pT.inc(tp)
                    bcsrc = recl_f[:, j * 16:(j + 1) * 16].rearrange(
                        "p (h o) -> p h o", o=1).broadcast_to((128, 16, 64))
                    cb = 8 * sl
                    g_div[j] = V(lambda bcsrc=bcsrc, sl=sl, cb=cb:
                                 VE.scalar_tensor_tensor(
                        ats[sl][:, :].rearrange("p (h o) -> p h o", h=16),
                        tb[:, 0:1024].rearrange("p (h o) -> p h o", h=16),
                        1.0, bcsrc, op0=ALU.mult, op1=ALU.mult,
                        accum_out=stat[:, cb:cb + 1]),
                        (pT, pT.n), (pV, v_recl))
                    vs2[j] = S(lambda sl=sl, cb=cb: SE.activation(
                        junk[:, :], ats[sl][:, :], AF.Square, bias=0.0,
                        scale=1.0, accum_out=stat[:, cb + 1:cb + 2]),
                        (pV, g_div[j]))

                def b4_G(j):
                    sl = j % 2
                    x1 = x1s[j % 3]
                    cb, cb3 = 8 * sl, 16 + 8 * sl
                    vr2 = ln_rstd(cb, ln_mid(cb, vs2[j]))
                    rbc = stat[:, cb + 5:cb + 6].rearrange(
                        "p (a o) -> p a o", o=1).broadcast_to((128, 1, D))
                    V(lambda x1=x1, sl=sl, cb=cb, cb3=cb3, rbc=rbc:
                      VE.scalar_tensor_tensor(
                        x1[:, :].rearrange("p (a o) -> p a o", a=1),
                        ats[sl][:, :].rearrange("p (a o) -> p a o", a=1),
                        stat[:, cb + 2:cb + 3], rbc,
                        op0=ALU.subtract, op1=ALU.mult,
                        accum_out=stat[:, cb3 + 6:cb3 + 7]),
                      (pS, vr2), (dxs[sl], xv[j]))
                    vv = V(lambda x1=x1, sl=sl: VE.tensor_add(
                        x1[:, :], x1[:, :], x_t[:, sl * D:(sl + 1) * D]))
                    xg[f"x{sl}"] = (pV, vv)
                    V(lambda cb3=cb3, j=j: VE.tensor_add(
                        stat[:, cb3:cb3 + 1], stat[:, cb3 + 6:cb3 + 7],
                        sums_x[:, j:j + 1]))
                    vs3[j] = S(lambda x1=x1, cb3=cb3: SE.activation(
                        junk[:, :], x1[:, :], AF.Square, bias=0.0, scale=1.0,
                        accum_out=stat[:, cb3 + 1:cb3 + 2]), (pV, vv))

                def b4_H(j):
                    sl = j % 2
                    x1 = x1s[j % 3]
                    vr3 = ln_rstd(16 + 8 * sl, ln_mid(16 + 8 * sl, vs3[j]))
                    v_n2 = ln_apply(x1[:, :], n1_t[:, sl * D:(sl + 1) * D],
                                    16 + 8 * sl, vr3,
                                    (pT, g_tpN.get(j - 2, 0)))
                    vv = g_evN.get(j - 1, 0)
                    if vv:
                        TE.wait_ge(pV.h, vv)
                    TW((pV, v_n2))
                    for dt in range(DT):
                        tp = TE.transpose(
                            tb[:, 1024 + dt * 128:1024 + dt * 128 + 128],
                            n1_t[:, sl * D + dt * 128:sl * D + dt * 128 + 128],
                            id_bf[:, :])
                    pT.inc(tp)
                    g_tpN[j] = pT.n
                    g_evN[j] = V(lambda sl=sl: VE.tensor_copy(
                        n2s[sl][:, :], tb[:, 1024:2048]), (pT, pT.n),
                        (pT, g_yT.get(j - 2, 0)))
                    for oc in range(2):
                        def mk(ps, oc=oc, sl=sl):
                            for dt in range(DT):
                                TE.matmul(
                                    ps,
                                    n2s[sl][:, dt * 128:dt * 128 + 128],
                                    m_ld[:, dt * D + oc * 512:
                                         dt * D + oc * 512 + 512],
                                    start=(dt == 0), stop=False)
                            return TE.matmul(ps, on64[0:1, :],
                                             bias_bf[0:1, oc * 512:
                                                     oc * 512 + 512],
                                             start=False, stop=True)

                        def ev(ps, tv, oc=oc, sl=sl):
                            cb4 = 32 + 8 * sl
                            vv = S(lambda: SE.activation(
                                yts[sl][:, oc * 512:oc * 512 + 512], ps,
                                AF.Identity, bias=0.0, scale=1.0,
                                accum_out=stat[:, cb4 + 6 + oc:cb4 + 7 + oc]),
                                (pT, tv))
                            return (pS, vv)
                        w0 = (pV, g_evN[j])
                        w1 = (dwm, v_mld) if (j == 0 and oc == 0) else (pV, 0)
                        yv = pj_chain(mk, ev, w0, w1)
                    g_yT[j] = pT.n
                    vs4[j] = S(lambda sl=sl: SE.activation(
                        junk[:, :], yts[sl][:, :], AF.Square, bias=0.0,
                        scale=1.0, accum_out=stat[:, 32 + 8 * sl + 1:
                                                  32 + 8 * sl + 2]), (pS, yv))

                def b4_I(j):
                    sl = j % 2
                    x1 = x1s[j % 3]
                    cb4 = 32 + 8 * sl
                    V(lambda cb4=cb4: VE.tensor_add(
                        stat[:, cb4:cb4 + 1], stat[:, cb4 + 6:cb4 + 7],
                        stat[:, cb4 + 7:cb4 + 8]), (pS, vs4[j]))
                    vr4 = ln_rstd(cb4, ln_mid(cb4, vs4[j]))
                    sv, vv = xg.get(f"x2{sl}", (dos[sl], 0))
                    x2a = x2_t[:, sl * D:(sl + 1) * D]
                    ln_apply(yts[sl][:, :], x2a, 32 + 8 * sl, vr4,
                             (sv, vv) if vv else (pV, 0))
                    v_x2 = V(lambda x2a=x2a, x1=x1: VE.tensor_add(
                        x2a, x2a, x1[:, :]))
                    SY.wait_ge(pV.h, v_x2)
                    st = SY.dma_start(out_d[b * t_len + j * 128:
                                            b * t_len + j * 128 + 128, :],
                                      x2a)
                    dos[sl].inc(st, 16)
                    xg[f"x2{sl}"] = (dos[sl], dos[sl].n)

                for r in range(TT + 3):
                    if r < TT:
                        b4_F(r)
                    if 0 <= r - 1 < TT:
                        b4_G(r - 1)
                    if 0 <= r - 2 < TT:
                        b4_H(r - 2)
                    if 0 <= r - 3 < TT:
                        b4_I(r - 3)
                xg["psum_b67"] = (pV, pV.n)
                xg["n1T_w"] = (pT, pT.n)
                xg["slotA"] = (pV, pV.n)
                xg["slotA2"] = (pT, pT.n)
                xg["pt_w"] = (pV, pV.n)

        SY.wait_ge(dos[0].h, dos[0].n)
        SY.wait_ge(dos[1].h, dos[1].n)
        SY.wait_ge(dxs[0].h, dxs[0].n)
        SY.wait_ge(dxs[1].h, dxs[1].n)
        SY.wait_ge(dtp.h, dtp.n)
        SY.wait_ge(dms.h, dms.n)
        SY.wait_ge(dsm.h, dsm.n)
        GP.wait_ge(dwA.h, dwA.n)
        GP.wait_ge(dwB.h, dwB.n)
        GP.wait_ge(dwc.h, dwc.n)
        GP.wait_ge(dwm.h, dwm.n)
        GP.wait_ge(dd.h, dd.n)

    nc.finalize()
    return nc


_HOST = {}


def _host_consts():
    if "cc" not in _HOST:
        k = np.arange(D, dtype=np.int64)
        ang = (2.0 * np.pi / D) * ((np.outer(k, k) % D).astype(np.float64))
        _HOST["cc"] = np.ascontiguousarray(np.cos(ang).astype(np.float32))
        _HOST["ss"] = np.ascontiguousarray(np.sin(ang).astype(np.float32))
        _HOST["id"] = np.eye(128, dtype=np.float32)
        e65 = np.zeros((128, 65), dtype=np.float32)
        e65[:64, 0] = 1.0
        e65[64:, 64] = 1.0
        _HOST["e65"] = e65
        _HOST["on1"] = np.ones((1, 128), dtype=np.float32)
    return _HOST


def kernel(L0_attn_x, L0_attn_Wq, L0_attn_Wk, L0_attn_Wv,
           L0_ffn_w_freq_real, L0_ffn_w_freq_imag, L0_ffn_bias):
    x = np.ascontiguousarray(np.asarray(L0_attn_x, dtype=np.float32))
    B, T, _ = x.shape
    b_loc = B // NCORES
    hc = _host_consts()
    base = dict(
        wqT=np.ascontiguousarray(np.asarray(L0_attn_Wq, np.float32).T),
        wkT=np.ascontiguousarray(np.asarray(L0_attn_Wk, np.float32).T),
        wvT=np.ascontiguousarray(np.asarray(L0_attn_Wv, np.float32).T),
        wr=np.ascontiguousarray(np.asarray(L0_ffn_w_freq_real, np.float32)),
        wi=np.ascontiguousarray(np.asarray(L0_ffn_w_freq_imag, np.float32)),
        bias=np.ascontiguousarray(np.asarray(L0_ffn_bias, np.float32)
                                  .reshape(1, D)),
        cc=hc["cc"], ssn=hc["ss"], idm=hc["id"], e65=hc["e65"], on1=hc["on1"],
    )
    nc = build_nc(b_loc=b_loc, t_len=T)
    in_maps = []
    for c in range(NCORES):
        m = dict(base)
        m["x"] = np.ascontiguousarray(
            x[c * b_loc:(c + 1) * b_loc].reshape(b_loc * T, D))
        in_maps.append(m)
    res = run_bass_kernel_spmd(nc, in_maps, core_ids=list(range(NCORES)))
    outs = [np.asarray(r["out"]).reshape(b_loc, T, D) for r in res.results]
    return np.concatenate(outs, axis=0)


# version 15
# speedup vs baseline: 1.2836x; 1.2836x over previous
import numpy as np

import concourse.bass as bass
import concourse.mybir as mybir
from concourse.bass_utils import run_bass_kernel_spmd

F32 = mybir.dt.float32
BF16 = mybir.dt.bfloat16
AF = mybir.ActivationFunctionType
ALU = mybir.AluOpType
AX = mybir.AxisListType

D = 1024
H = 16
DT = 8
NCORES = 8
EPS = 1e-5
ISCALE = 1.0 / 32.0


class Sem:
    def __init__(self, nc, name):
        self.h = nc.semaphore(name).__enter__()
        self.n = 0

    def inc(self, instr, k=1):
        instr.then_inc(self.h, k)
        self.n += k
        return self.n


def chunks(total, size):
    out, off = [], 0
    while off < total:
        out.append((off, min(size, total - off)))
        off += size
    return out


def build_nc(b_loc=2, t_len=2048):
    TT = t_len // 128
    KT = t_len // 128
    QH = min(1024, t_len)
    NH = t_len // QH

    nc = bass.Bass("TRN2", target_bir_lowering=False)

    x_d = nc.dram_tensor("x", [b_loc * t_len, D], F32, kind="ExternalInput")
    wqT_d = nc.dram_tensor("wqT", [D, D], F32, kind="ExternalInput")
    wkT_d = nc.dram_tensor("wkT", [D, D], F32, kind="ExternalInput")
    wvT_d = nc.dram_tensor("wvT", [D, D], F32, kind="ExternalInput")
    wr_d = nc.dram_tensor("wr", [D, D], F32, kind="ExternalInput")
    wi_d = nc.dram_tensor("wi", [D, D], F32, kind="ExternalInput")
    bias_d = nc.dram_tensor("bias", [1, D], F32, kind="ExternalInput")
    cc_d = nc.dram_tensor("cc", [D, D], F32, kind="ExternalInput")
    ssn_d = nc.dram_tensor("ssn", [D, D], F32, kind="ExternalInput")
    id_d = nc.dram_tensor("idm", [128, 128], F32, kind="ExternalInput")
    e65_d = nc.dram_tensor("e65", [128, 65], F32, kind="ExternalInput")
    on1_d = nc.dram_tensor("on1", [1, 128], F32, kind="ExternalInput")
    out_d = nc.dram_tensor("out", [b_loc * t_len, D], F32, kind="ExternalOutput")
    dnm_d = nc.dram_tensor("dnm", [H, t_len], BF16, kind="Internal")
    m_d = nc.dram_tensor("m_hbm", [128, DT * D], BF16, kind="Internal")

    def wtiles(d):
        return d[:, :].rearrange("(t p) f -> p t f", p=128)

    ARENA = 202 * 1024
    _arena = nc.alloc_sbuf_tensor("arena", [128, ARENA], mybir.dt.uint8)
    _ab = nc.lookup_mloc(_arena).addr
    base = [_ab]
    amap = {}

    def A(name, shape, dtype, at=None):
        size = int(np.prod(shape[1:])) * (2 if dtype == BF16 else 4)
        size = (size + 31) // 32 * 32
        off = base[0] if at is None else at
        if at is None:
            base[0] += size
        amap[name] = off
        return nc.alloc_sbuf_tensor_at(name, list(shape), dtype, offset=off)

    r1_pad = max(DT * t_len * 2, 2 * DT * D * 2)
    n1T = A("n1T", [128, DT * t_len], BF16)
    if r1_pad > DT * t_len * 2:
        base[0] += r1_pad - DT * t_len * 2
    attnT = A("attnT", [128, DT * t_len], BF16)
    if r1_pad > DT * t_len * 2:
        base[0] += r1_pad - DT * t_len * 2
    vaug = A("vaug", [128, KT * (H * 65)], BF16)
    w_bf = A("w_bf", [128, 2 * DT * D], BF16)
    qT_b = A("qT_b", [128, 2 * t_len], BF16)
    kT_b = A("kT_b", [128, 2 * t_len], BF16)
    pt_sb = A("pt_sb", [128, 2 * QH], BF16)
    if 2 * QH * 2 < 4096:
        base[0] += 4096 - 2 * QH * 2
    qsq = A("qsq", [128, t_len], BF16)
    if t_len * 2 < 4096:
        base[0] += 4096 - t_len * 2
    x_t = A("x_t", [128, 2 * D], F32)
    id_bf = A("id_bf", [128, 128], BF16)
    n1_t = A("n1_t", [128, 2 * D], BF16)
    a_tok = A("a_tok", [128, D], F32)
    tmp_t = A("tmp_t", [128, D], F32)
    x1_t = A("x1_t", [128, D], F32)
    x2_t = A("x2_t", [128, 2 * D], F32)
    n2T_t = A("n2T_t", [128, D], BF16)
    m_stage = A("m_stage", [128, 512], BF16)
    recl = A("recl", [128, TT * 16], BF16)
    recl_f = A("recl_f", [128, TT * 16], F32)
    rex = A("rex", [128, D], F32)
    e65_bf = A("e65_bf", [128, 65], BF16)
    on1_f = A("on1_f", [1, 128], F32)
    on64 = A("on64", [65, 128], BF16)
    negb = A("negb", [128, H], F32)
    nb_bf = A("nb_bf", [65, 1], BF16)
    stat = A("stat", [128, 80], F32)
    st2 = A("st2", [65, 12], F32)
    eps_t = A("eps_t", [128, 1], F32)
    fence_t = A("fence_t", [128, 1], F32)
    sums_x = A("sums_x", [128, 16], F32)
    m_ld = nc.alloc_sbuf_tensor_at("m_ld", [128, DT * D], BF16,
                                   offset=amap["n1T"])
    p1T = nc.alloc_sbuf_tensor_at("p1T", [128, DT * D], BF16,
                                  offset=amap["n1T"])
    p2T = nc.alloc_sbuf_tensor_at("p2T", [128, DT * D], BF16,
                                  offset=amap["n1T"] + DT * D * 2)
    cc_bf = nc.alloc_sbuf_tensor_at("cc_bf", [128, DT * D], BF16,
                                    offset=amap["attnT"])
    ss_bf = nc.alloc_sbuf_tensor_at("ss_bf", [128, DT * D], BF16,
                                    offset=amap["attnT"] + DT * D * 2)
    stage = nc.alloc_sbuf_tensor_at("stage", [1, t_len], F32,
                                    offset=amap["x_t"])
    bias_s = nc.alloc_sbuf_tensor_at("bias_s", [1, D], F32, offset=amap["rex"])
    x1b1 = nc.alloc_sbuf_tensor_at("x1b1", [128, D], F32, offset=amap["rex"])
    x1b2 = nc.alloc_sbuf_tensor_at("x1b2", [128, D], F32, offset=amap["w_bf"])
    n2Tb = nc.alloc_sbuf_tensor_at("n2Tb", [128, D], BF16,
                                   offset=amap["w_bf"] + 4096)
    junk = nc.alloc_sbuf_tensor_at("junk", [128, D], BF16,
                                   offset=amap["w_bf"] + 6144)
    x1b4 = nc.alloc_sbuf_tensor_at("x1b4", [128, D], F32,
                                   offset=amap["w_bf"] + 8192)
    atok3 = nc.alloc_sbuf_tensor_at("atok3", [128, D], F32,
                                    offset=amap["w_bf"] + 12288)
    y_t0 = nc.alloc_sbuf_tensor_at("y_t0", [128, D], F32, offset=amap["pt_sb"])
    y_t1 = nc.alloc_sbuf_tensor_at("y_t1", [128, D], F32, offset=amap["qsq"])
    bias_bf = A("bias_bf", [1, D], BF16)
    assert base[0] <= _ab + ARENA, f"SBUF over: {base[0]-_ab}"

    pT = Sem(nc, "pT")
    pV = Sem(nc, "pV")
    pS = Sem(nc, "pS")
    dxs = [Sem(nc, "dx0"), Sem(nc, "dx1")]
    dsm = Sem(nc, "dsm")
    dwA = Sem(nc, "dwA")
    dwB = Sem(nc, "dwB")
    dwc = Sem(nc, "dwc")
    dwm = Sem(nc, "dwm")
    dms = Sem(nc, "dms")
    dd = Sem(nc, "dd")
    dtp = Sem(nc, "dtp")
    dos = [Sem(nc, "do0"), Sem(nc, "do1")]

    TE, VE, SE, SY, GP = nc.tensor, nc.vector, nc.scalar, nc.sync, nc.gpsimd

    def V(fn, *waits):
        for (s, v) in waits:
            if v:
                VE.wait_ge(s.h, v)
        VE.wait_ge(pV.h, pV.n)
        return pV.inc(fn())

    def S(fn, *waits):
        for (s, v) in waits:
            if v:
                SE.wait_ge(s.h, v)
        SE.wait_ge(pS.h, pS.n)
        return pS.inc(fn())

    def TW(*waits):
        for (s, v) in waits:
            if v:
                TE.wait_ge(s.h, v)

    xg = {}

    def g_wait_T(key):
        s, v = xg.get(key, (pT, 0))
        if v:
            TE.wait_ge(s.h, v)

    def ln_begin(src_ap, cb, sum_on="S", src_w=None):
        w = [src_w] if src_w else []
        if sum_on == "V":
            V(lambda: VE.reduce_sum(stat[:, cb:cb + 1], src_ap, axis=AX.X),
              *w)
        else:
            S(lambda: SE.activation(junk[:, :], src_ap, AF.Identity, bias=0.0,
                                    scale=1.0, accum_out=stat[:, cb:cb + 1]),
              *w)
        return S(lambda: SE.activation(junk[:, :], src_ap, AF.Square,
                                       bias=0.0, scale=1.0,
                                       accum_out=stat[:, cb + 1:cb + 2]), *w)

    def ln_mid(cb, vb, sum_ap=None):
        sap = stat[:, cb:cb + 1] if sum_ap is None else sum_ap
        V(lambda: VE.tensor_scalar_mul(stat[:, cb + 2:cb + 3],
                                       sap, 1.0 / D), (pS, vb))
        V(lambda: VE.tensor_mul(stat[:, cb + 3:cb + 4], stat[:, cb + 2:cb + 3],
                                stat[:, cb + 2:cb + 3]))
        return V(lambda: VE.tensor_scalar(stat[:, cb + 4:cb + 5],
                                          stat[:, cb + 1:cb + 2], 1.0 / D,
                                          stat[:, cb + 3:cb + 4],
                                          op0=ALU.mult, op1=ALU.subtract))

    def ln_rstd(cb, vmid):
        S(lambda: SE.activation(stat[:, cb + 5:cb + 6], stat[:, cb + 4:cb + 5],
                                AF.Ln, bias=eps_t[:, :], scale=1.0),
          (pV, vmid))
        return S(lambda: SE.activation(stat[:, cb + 5:cb + 6],
                                       stat[:, cb + 5:cb + 6], AF.Exp,
                                       bias=0.0, scale=-0.5))

    def ln_stats(src_ap, cb, mode="VS", src_w=None):
        vb = ln_begin(src_ap, cb, sum_on=mode[0], src_w=src_w)
        return ln_rstd(cb, ln_mid(cb, vb))

    def ln_apply(src_ap, out_ap, cb, vs, *waits):
        return V(lambda: VE.tensor_scalar(out_ap, src_ap,
                                          stat[:, cb + 2:cb + 3],
                                          stat[:, cb + 5:cb + 6],
                                          op0=ALU.subtract, op1=ALU.mult),
                 (pS, vs), *waits)

    def ln_block(src_ap, out_ap, residual_ap):
        V(lambda: VE.reduce_sum(stat[:, 0:1], src_ap, axis=AX.X))
        V(lambda: VE.scalar_tensor_tensor(tmp_t[:, :], src_ap, 1.0, src_ap,
                                          op0=ALU.mult, op1=ALU.mult,
                                          accum_out=stat[:, 1:2]))
        V(lambda: VE.tensor_scalar_mul(stat[:, 2:3], stat[:, 0:1], 1.0 / D))
        V(lambda: VE.tensor_mul(stat[:, 3:4], stat[:, 2:3], stat[:, 2:3]))
        vv = V(lambda: VE.tensor_scalar(stat[:, 4:5], stat[:, 1:2], 1.0 / D,
                                        stat[:, 3:4], op0=ALU.mult,
                                        op1=ALU.subtract))
        S(lambda: SE.activation(stat[:, 5:6], stat[:, 4:5], AF.Ln,
                                bias=eps_t[:, :], scale=1.0), (pV, vv))
        vs = S(lambda: SE.activation(stat[:, 5:6], stat[:, 5:6], AF.Exp,
                                     bias=0.0, scale=-0.5))
        if residual_ap is None:
            return V(lambda: VE.tensor_scalar(out_ap, src_ap, stat[:, 2:3],
                                              stat[:, 5:6], op0=ALU.subtract,
                                              op1=ALU.mult), (pS, vs))
        V(lambda: VE.tensor_scalar(tmp_t[:, :], src_ap, stat[:, 2:3],
                                   stat[:, 5:6], op0=ALU.subtract,
                                   op1=ALU.mult), (pS, vs))
        return V(lambda: VE.tensor_add(out_ap, tmp_t[:, :], residual_ap))

    with (
        nc.psum_tensor("pv_ps", [128, 1024], F32) as pv_ps,
        nc.psum_tensor("pj_ps", [128, 1024], F32) as pj_ps,
        nc.psum_tensor("s0_ps", [128, 1024], F32) as s0_ps,
    ):
        pjg = [(pV, 0), (pV, 0)]
        pj_par = [0]

        def pj_chain(emit_mms, evac, *mm_waits):
            w = pj_par[0]
            s, v = pjg[w]
            if v:
                TE.wait_ge(s.h, v)
            TW(*mm_waits)
            last = emit_mms(pj_ps[:, w * 512:(w + 1) * 512])
            pT.inc(last)
            g = evac(pj_ps[:, w * 512:(w + 1) * 512], pT.n)
            if not isinstance(g, tuple):
                g = (pV, g)
            pjg[w] = g
            pj_par[0] ^= 1
            return g[1]

        dwc.inc(GP.dma_start(id_bf[:, :], id_d[:, :]), 16)
        dwc.inc(GP.dma_start(e65_bf[:, :], e65_d[:, :]), 16)
        dwc.inc(GP.dma_start(on64[0:1, :], on1_d[:, :]), 16)
        dwc.inc(GP.dma_start(on64[64:65, :], on1_d[:, :]), 16)
        for t in range(DT):
            dwc.inc(GP.dma_start(cc_bf[:, t * D:(t + 1) * D],
                                 wtiles(cc_d)[:, t, :]), 16)
        for t in range(DT):
            dwc.inc(GP.dma_start(ss_bf[:, t * D:(t + 1) * D],
                                 wtiles(ssn_d)[:, t, :]), 16)
        for t in range(DT):
            dwA.inc(GP.dma_start(w_bf[:, t * D:(t + 1) * D],
                                 wtiles(wr_d)[:, t, :]), 16)
        for t in range(DT):
            dwB.inc(GP.dma_start(w_bf[:, (DT + t) * D:(DT + t + 1) * D],
                                 wtiles(wi_d)[:, t, :]), 16)
        v_setup_c, v_setup_a, v_setup_b = dwc.n, dwA.n, dwB.n
        dsm.inc(SY.dma_start(bias_s[:, :], bias_d[:, :]), 16)
        dsm.inc(SY.dma_start(on1_f[:, :], on1_d[:, :]), 16)
        v_small = dsm.n

        V(lambda: VE.memset(eps_t[:, :], EPS))
        va = vaug[:, :].rearrange("p (k c) -> p k c", c=65)
        V(lambda: VE.memset(va[:, :, 64:65], 1.0))

        V(lambda: VE.tensor_copy(bias_bf[:, :], bias_s[:, :]),
          (dsm, v_small))

        TW((dwc, v_setup_c), (dwA, v_setup_a), (dwB, v_setup_b))

        def emit_mat(dst, pairs, spill=False):
            for i in range(DT):
                for oc in range(2):
                    def mk(ps, i=i, oc=oc):
                        first = True
                        np_ = len(pairs)
                        for pi, (lw, lblk, rt) in enumerate(pairs):
                            for dt in range(DT):
                                mm = TE.matmul(
                                    ps,
                                    lw[:, lblk + dt * D + i * 128:
                                       lblk + dt * D + i * 128 + 128],
                                    rt[:, dt * D + oc * 512:
                                       dt * D + oc * 512 + 512],
                                    start=first,
                                    stop=(pi == np_ - 1 and dt == DT - 1))
                                first = False
                        return mm

                    if not spill:
                        def ev(ps, tv, i=i, oc=oc):
                            return V(lambda: VE.tensor_copy(
                                dst[:, i * D + oc * 512:i * D + oc * 512 + 512],
                                ps), (pT, tv))
                    else:
                        def ev(ps, tv, i=i, oc=oc):
                            vv = V(lambda: VE.tensor_scalar_mul(
                                m_stage[:, :], ps, 1.0 / D), (pT, tv),
                                xg.get("m_stage", (pV, 0)))
                            SY.wait_ge(pV.h, vv)
                            st = SY.dma_start(
                                m_d[:, (i * 2 + oc) * 512:
                                    (i * 2 + oc + 1) * 512], m_stage[:, :])
                            dms.inc(st, 16)
                            xg["m_stage"] = (dms, dms.n)
                            return vv
                    pj_chain(mk, ev)

        emit_mat(p1T, [(w_bf, 0, cc_bf), (w_bf, DT * D, ss_bf)])
        v_sneg = V(lambda: VE.tensor_scalar_mul(ss_bf[:, :], ss_bf[:, :], -1.0),
                   (pT, pT.n))
        TW((pV, v_sneg))
        emit_mat(p2T, [(w_bf, DT * D, cc_bf), (w_bf, 0, ss_bf)])
        TW((pV, pV.n))
        emit_mat(None, [(p1T, 0, cc_bf), (p2T, 0, ss_bf)], spill=True)
        v_m_T = pT.n
        v_m_spill = dms.n
        xg["n1T_w"] = (pT, v_m_T)
        xg["attnT_w"] = (pT, v_m_T)
        xg["slotA"] = (pT, v_m_T)
        xg["slotA2"] = (pT, v_m_T)

        GP.wait_ge(pT.h, v_m_T)
        for t in range(DT):
            dwB.inc(GP.dma_start(w_bf[:, (DT + t) * D:(DT + t + 1) * D],
                                 wtiles(wkT_d)[:, t, :]), 16)
        v_wk = dwB.n

        for b in range(b_loc):
            with nc.psum_tensor(f"tb1_{b}", [128, 2048], BF16) as tb:
                g_wait_T("psum_b67")
                V(lambda: VE.memset(fence_t[0:1, 0:1], 0.0), (pT, pT.n))
                tbg = [(pV, 0), (pV, 0)]
                first_evac = [True]
                vs1 = {}
                xv1 = {}
                g_tp1 = {}

                def b1_F(j):
                    sl = j % 2
                    sv, vv = xg.get(f"x{sl}", (pV, 0))
                    if vv:
                        SY.wait_ge(sv.h, vv)
                    sv, vv = xg.get("stage_rd", (dd, 0))
                    if vv and j < 2:
                        SY.wait_ge(sv.h, vv)
                    ld = SY.dma_start(x_t[:, sl * D:(sl + 1) * D],
                                      x_d[b * t_len + j * 128:
                                          b * t_len + j * 128 + 128, :])
                    dxs[sl].inc(ld, 16)
                    xv1[j] = dxs[sl].n
                    V(lambda sl=sl, j=j: VE.reduce_sum(
                        sums_x[:, j:j + 1], x_t[:, sl * D:(sl + 1) * D],
                        axis=AX.X), (dxs[sl], xv1[j]))
                    vs1[j] = S(lambda sl=sl: SE.activation(
                        junk[:, :], x_t[:, sl * D:(sl + 1) * D], AF.Square,
                        bias=0.0, scale=1.0,
                        accum_out=stat[:, 8 * sl + 1:8 * sl + 2]),
                        (dxs[sl], xv1[j]))

                def b1_G(j):
                    sl = j % 2
                    vr1 = ln_rstd(8 * sl, ln_mid(8 * sl, vs1[j],
                                                 sums_x[:, j:j + 1]))
                    v_ap = ln_apply(x_t[:, sl * D:(sl + 1) * D],
                                    n1_t[:, sl * D:(sl + 1) * D], 8 * sl,
                                    vr1, (pT, g_tp1.get(j - 2, 0)))
                    xg[f"x{sl}"] = (pV, v_ap)
                    sv, vv = tbg[sl]
                    if vv:
                        TE.wait_ge(sv.h, vv)
                    TW((pV, v_ap))
                    for dt in range(DT):
                        tp = TE.transpose(
                            tb[:, sl * 1024 + dt * 128:sl * 1024 + dt * 128 + 128],
                            n1_t[:, sl * D + dt * 128:sl * D + dt * 128 + 128],
                            id_bf[:, :])
                    pT.inc(tp)
                    g_tp1[j] = pT.n
                    dst = n1T[:, :].rearrange("p (t q) -> p t q", q=t_len)[
                        :, :, j * 128:j * 128 + 128]
                    src = tb[:, sl * 1024:sl * 1024 + 1024].rearrange(
                        "p (t q) -> p t q", q=128)
                    ex = (xg["n1T_w"],) if first_evac[0] else ()
                    first_evac[0] = False
                    vv = V(lambda dst=dst, src=src: VE.tensor_copy(dst, src),
                           (pT, pT.n), *ex)
                    tbg[sl] = (pV, vv)

                for r in range(TT + 1):
                    if r < TT:
                        b1_F(r)
                    if r - 1 >= 0:
                        b1_G(r - 1)
                v_b1 = pV.n
                xg["psum_b67"] = (pV, pV.n)
                xg["slotA"] = (pV, pV.n)

            with nc.psum_tensor(f"s1_{b}", [128, 1024], F32) as s1_ps:
                g_wait_T("psum_b67")
                g = xg["slotA"]
                GP.wait_ge(g[0].h, g[1])
                g = xg.get("slotA2", (pT, 0))
                if g[1]:
                    GP.wait_ge(g[0].h, g[1])
                for t in range(DT):
                    dwA.inc(GP.dma_start(w_bf[:, t * D:(t + 1) * D],
                                         wtiles(wvT_d)[:, t, :]), 16)
                v_wv = dwA.n
                vaug_g = xg.get("vaug_w", (pT, 0))
                first_vevac = [True]
                for j in range(TT):
                    for oc in range(2):
                        def mk(ps, j=j, oc=oc):
                            for dt in range(DT):
                                mm = TE.matmul(
                                    ps,
                                    n1T[:, dt * t_len + j * 128:
                                        dt * t_len + j * 128 + 128],
                                    w_bf[:, dt * D + oc * 512:
                                         dt * D + oc * 512 + 512],
                                    start=(dt == 0), stop=(dt == DT - 1))
                            return mm

                        def ev(ps, tv, j=j, oc=oc):
                            dstv = vaug[:, j * (H * 65):
                                        (j + 1) * (H * 65)].rearrange(
                                "p (h c) -> p h c", c=65)[
                                :, oc * 8:(oc + 1) * 8, 0:64]
                            srcv = ps.rearrange("p (h c) -> p h c", c=64)
                            ex = (vaug_g,) if first_vevac[0] else ()
                            first_vevac[0] = False
                            return V(lambda: VE.tensor_copy(dstv, srcv),
                                     (pT, tv), *ex)
                        w0 = (dwA, v_wv) if (j == 0 and oc == 0) else (pV, 0)
                        w1 = (pV, v_b1) if (j == 0 and oc == 0) else (pV, 0)
                        pj_chain(mk, ev, w0, w1)
                GP.wait_ge(pT.h, pT.n)
                for t in range(DT):
                    dwA.inc(GP.dma_start(w_bf[:, t * D:(t + 1) * D],
                                         wtiles(wqT_d)[:, t, :]), 16)
                xg["v_wq"] = dwA.n

                def bg_pair(p):
                    par = p % 2

                    def mkproj_a(wbase, tc, ln):
                        def go():
                            w = pj_par[0]
                            sg_, vg_ = pjg[w]
                            if vg_:
                                TE.wait_ge(sg_.h, vg_)
                            w0 = (dwA, xg["v_wq"]) if wbase == 0 else \
                                (dwB, v_wk)
                            TW(w0)
                            for dt in range(DT // 2):
                                TE.matmul(
                                    pj_ps[:, w * 512:w * 512 + ln],
                                    w_bf[:, wbase + dt * D + p * 128:
                                         wbase + dt * D + p * 128 + 128],
                                    n1T[:, dt * t_len + tc:
                                        dt * t_len + tc + ln],
                                    start=(dt == 0), stop=False)
                        return go

                    def mkproj_b(wbase, tc, ln):
                        def go():
                            w = pj_par[0]
                            qkg = xg.get(f"qk{par}", (pT, 0))
                            for dt in range(DT // 2, DT):
                                mm = TE.matmul(
                                    pj_ps[:, w * 512:w * 512 + ln],
                                    w_bf[:, wbase + dt * D + p * 128:
                                         wbase + dt * D + p * 128 + 128],
                                    n1T[:, dt * t_len + tc:
                                        dt * t_len + tc + ln],
                                    start=False, stop=(dt == DT - 1))
                            pT.inc(mm)
                            dstT = qT_b if wbase == 0 else kT_b
                            vv = V(lambda: VE.tensor_copy(
                                dstT[:, par * t_len + tc:
                                     par * t_len + tc + ln],
                                pj_ps[:, w * 512:w * 512 + ln]),
                                (pT, pT.n), qkg)
                            pjg[w] = (pV, vv)
                            pj_par[0] ^= 1
                        return go

                    for wbase in (0, DT * D):
                        for tc, ln in chunks(t_len, 512):
                            yield (4, mkproj_a(wbase, tc, ln))
                            yield (4, mkproj_b(wbase, tc, ln))

                    yield (0, lambda: xg.__setitem__("v_qkev", pV.n))

                    def mk_qsq(srcT):
                        def go():
                            xg["v_qsq"] = V(
                                lambda: VE.scalar_tensor_tensor(
                                    qsq[:, :],
                                    srcT[:, par * t_len:(par + 1) * t_len],
                                    1.0,
                                    srcT[:, par * t_len:(par + 1) * t_len],
                                    op0=ALU.mult, op1=ALU.mult),
                                (pV, xg["v_qkev"]), xg.get("qsq_rd", (pT, 0)),
                                xg.get("pt_w", (pV, 0)))
                        return go

                    def mk_norm(tc, ln, col):
                        def go():
                            def mk(ps):
                                return TE.matmul(ps[0:65, 0:ln], e65_bf[:, :],
                                                 qsq[:, tc:tc + ln],
                                                 start=True, stop=True)

                            def ev(ps, tv):
                                vv = V(lambda: VE.reduce_max(
                                    st2[:, col:col + 1], ps[0:65, 0:ln],
                                    axis=AX.X), (pT, tv))
                                xg["qsq_rd"] = (pT, tv)
                                return vv
                            pj_chain(mk, ev, (pV, xg["v_qsq"]))
                        return go

                    yield (0, mk_qsq(qT_b))
                    for ci, (tc, ln) in enumerate(chunks(t_len, 512)):
                        yield (1, mk_norm(tc, ln, ci))
                    yield (0, mk_qsq(kT_b))
                    for ci, (tc, ln) in enumerate(chunks(t_len, 512)):
                        yield (1, mk_norm(tc, ln, 4 + ci))

                    def mk_fin():
                        def go():
                            nch = len(chunks(t_len, 512))
                            V(lambda: VE.tensor_reduce(
                                st2[:, 8:9], st2[:, 0:nch], axis=AX.X,
                                op=ALU.max))
                            V(lambda: VE.tensor_reduce(
                                st2[:, 9:10], st2[:, 4:4 + nch], axis=AX.X,
                                op=ALU.max))
                            vv = V(lambda: VE.tensor_mul(
                                st2[:, 10:11], st2[:, 8:9], st2[:, 9:10]))
                            S(lambda: SE.activation(
                                st2[:, 11:12], st2[:, 10:11], AF.Ln,
                                bias=eps_t[0:65, :], scale=1.0), (pV, vv))
                            vs = S(lambda: SE.activation(
                                st2[:, 11:12], st2[:, 11:12], AF.Exp,
                                bias=0.0, scale=0.5))
                            vn = V(lambda: VE.tensor_scalar_mul(
                                nb_bf[:, :], st2[:, 11:12], -1.0 / 32.0),
                                (pS, vs))

                            def mk(ps):
                                for h2 in range(2):
                                    mm = TE.matmul(
                                        ps[0:128, h2:h2 + 1],
                                        on64[64 * h2:64 * h2 + 1, :],
                                        nb_bf[64 * h2:64 * h2 + 1, :],
                                        start=True, stop=True)
                                return mm

                            def ev(ps, tv):
                                vv = V(lambda: VE.tensor_copy(
                                    negb[:, 2 * p:2 * p + 2],
                                    ps[0:128, 0:2]), (pT, tv))
                                xg[f"negb{p}"] = (pV, vv)
                                return vv
                            pj_chain(mk, ev, (pV, vn))
                        return go
                    yield (1, mk_fin())

                for _, go in bg_pair(0):
                    go()

                for p in range(H // 2):
                    par = p % 2
                    bg = list(bg_pair(p + 1)) if p + 1 < H // 2 else []
                    bgi = [0]
                    n_slots = 2 * NH * KT
                    bg_total = sum(c for c, _ in bg) or 1
                    bg_done = [0]

                    def bg_step():
                        while (bg and bgi[0] < len(bg)
                               and bg_done[0] / bg_total <
                               (slot[0] + 1) / n_slots):
                            c, go = bg[bgi[0]]
                            go()
                            bg_done[0] += c
                            bgi[0] += 1

                    v_qk_here = xg.get("v_qkev", pV.n)
                    slot = [0]
                    for h2 in range(2):
                        head = 2 * p + h2
                        hb = h2 * 64
                        ptg = xg.setdefault("ptg", [(pT, 0), (pT, 0)])
                        sg = [xg.get("psum_s0", (pS, 0)),
                              xg.get("psum_s1x", (pS, 0))]

                        def emit_S(kt, half, first):
                            sp = [s0_ps, s1_ps][kt % 2]
                            q0 = half * QH
                            s, v = sg[kt % 2]
                            if v:
                                TE.wait_ge(s.h, v)
                            if first:
                                TW((pV, v_qk_here))
                            for qc, ln in chunks(QH, 512):
                                smm = TE.matmul(
                                    sp[:, qc:qc + ln],
                                    kT_b[hb:hb + 64,
                                         par * t_len + kt * 128:
                                         par * t_len + kt * 128 + 128],
                                    qT_b[hb:hb + 64,
                                         par * t_len + q0 + qc:
                                         par * t_len + q0 + qc + ln],
                                    start=True, stop=True)
                            return pT.inc(smm)

                        for half in range(NH):
                            q0 = half * QH
                            v_smm_next = emit_S(0, half,
                                                half == 0 and h2 == 0)
                            for kt in range(KT):
                                sp = [s0_ps, s1_ps][kt % 2]
                                v_smm = v_smm_next
                                exw = [(pT, v_smm), ptg[kt % 2]]
                                if kt == 0 and half == 0:
                                    exw.append(xg.get(f"negb{p}", (pV, 0)))
                                    if h2 == 0 and p == 0:
                                        exw.append(xg.get("pt_w", (pV, 0)))
                                vexp = S(lambda kt=kt, sp=sp, head=head:
                                         SE.activation(
                                             pt_sb[:, (kt % 2) * QH:
                                                   (kt % 2) * QH + QH],
                                             sp[:, 0:QH], AF.Exp,
                                             bias=negb[:, head:head + 1],
                                             scale=ISCALE), *exw)
                                sg[kt % 2] = (pS, vexp)
                                if kt + 1 < KT:
                                    v_smm_next = emit_S(kt + 1, half, False)
                                bg_step()
                                slot[0] += 1
                                TW((pS, vexp))
                                if kt == 0:
                                    g_wait_T("psum_pv")
                                for qc, ln in chunks(QH, 512):
                                    pmm = TE.matmul(
                                        pv_ps[0:65, qc:qc + ln],
                                        vaug[:, kt * (H * 65) + head * 65:
                                             kt * (H * 65) + head * 65 + 65],
                                        pt_sb[:, (kt % 2) * QH + qc:
                                              (kt % 2) * QH + qc + ln],
                                        start=(kt == 0), stop=(kt == KT - 1))
                                pT.inc(pmm)
                                ptg[kt % 2] = (pT, pT.n)
                            dte = head // 2
                            aw = [(pT, pT.n)]
                            if b == 0 and head == 0 and half == 0:
                                aw.append(xg["attnT_w"])
                            S(lambda dte=dte, hb=hb, q0=q0: SE.activation(
                                attnT[hb:hb + 64,
                                      dte * t_len + q0:dte * t_len + q0 + QH],
                                pv_ps[0:64, 0:QH], AF.Identity, bias=0.0,
                                scale=1.0), *aw)
                            sw = [xg.get("stage_rd", (dd, 0))]
                            if head == 0 and half == 0:
                                sw += [xg.get("x0", (pV, 0)),
                                       xg.get("x1", (pV, 0))]
                            v_dn = V(lambda q0=q0: VE.tensor_copy(
                                stage[:, q0:q0 + QH], pv_ps[64:65, 0:QH]),
                                (pT, pT.n), *sw)
                            xg["psum_pv"] = (pV, v_dn)
                        xg["psum_s0"] = sg[0]
                        xg["psum_s1x"] = sg[1]
                        GP.wait_ge(pV.h, v_dn)
                        if head == 0:
                            s, v = xg.get("dnm_rd", (dtp, 0))
                            if v:
                                GP.wait_ge(s.h, v)
                        dn = GP.dma_start(dnm_d[head:head + 1, :], stage[:, :])
                        dd.inc(dn, 16)
                        xg["stage_rd"] = (dd, dd.n)
                    while bgi[0] < len(bg):
                        bg_step()
                    xg[f"qk{par}"] = (pT, pT.n)
                v_attn_T = pT.n
                v_attn_S = pS.n
                xg["vaug_w"] = (pT, pT.n)
                xg["psum_b67"] = (pV, pV.n)
                xg["psum_s1"] = xg.pop("psum_s1x")

            GP.wait_ge(pT.h, v_attn_T)
            GP.wait_ge(dms.h, v_m_spill)
            for i in range(4):
                dwm.inc(GP.dma_start(m_ld[:, i * 2048:(i + 1) * 2048],
                                     m_d[:, i * 2048:(i + 1) * 2048]), 16)
            v_mld = dwm.n
            SY.wait_ge(dd.h, dd.n)
            s, v = xg.get("recl_rd", (pV, 0))
            if v:
                SY.wait_ge(s.h, v)
            for j in range(TT):
                dtp.inc(SY.dma_start(recl[:, j * 16:(j + 1) * 16],
                                     dnm_d[0:16, j * 128:(j + 1) * 128],
                                     transpose=True), 16)
            xg["dnm_rd"] = (dtp, dtp.n)
            v_recl = V(lambda: VE.reciprocal(recl_f[:, :], recl[:, :]),
                       (dtp, dtp.n))
            xg["recl_rd"] = (pV, v_recl)

            with nc.psum_tensor(f"tb4_{b}", [128, 2048], BF16) as tb:
                g_wait_T("psum_b67")
                V(lambda: VE.memset(fence_t[0:1, 0:1], 0.0), (pT, v_attn_T))
                x1s = [x1_t, x1b1, x1b2, x1b4]
                ats = [a_tok, tmp_t, atok3]
                yts = [y_t0, y_t1]
                n2s = [n2T_t, n2Tb]
                vs2, vs3, vs4 = {}, {}, {}
                xv = {}
                g_div = {}
                g_tpN = {}
                g_yT = {}
                g_evN = {}

                def cb2(j):
                    return 8 * (j % 3)

                def cb3(j):
                    return 24 + 8 * (j % 3)

                def cb4(j):
                    return 48 + 8 * (j % 3)

                def b4_F(j):
                    sl = j % 3
                    vv = g_div.get(j - 1, 0)
                    if vv:
                        TE.wait_ge(pV.h, vv)
                    if j == 0:
                        TW((pS, v_attn_S))
                    for dt in range(DT):
                        tp = TE.transpose(
                            tb[:, dt * 128:dt * 128 + 128],
                            attnT[:, dt * t_len + j * 128:
                                  dt * t_len + j * 128 + 128],
                            id_bf[:, :])
                    pT.inc(tp)
                    bcsrc = recl_f[:, j * 16:(j + 1) * 16].rearrange(
                        "p (h o) -> p h o", o=1).broadcast_to((128, 16, 64))
                    cb = cb2(j)
                    g_div[j] = V(lambda bcsrc=bcsrc, sl=sl, cb=cb:
                                 VE.scalar_tensor_tensor(
                        ats[sl][:, :].rearrange("p (h o) -> p h o", h=16),
                        tb[:, 0:1024].rearrange("p (h o) -> p h o", h=16),
                        1.0, bcsrc, op0=ALU.mult, op1=ALU.mult,
                        accum_out=stat[:, cb:cb + 1]),
                        (pT, pT.n), (pV, v_recl))
                    vs2[j] = S(lambda sl=sl, cb=cb: SE.activation(
                        junk[:, :], ats[sl][:, :], AF.Square, bias=0.0,
                        scale=1.0, accum_out=stat[:, cb + 1:cb + 2]),
                        (pV, g_div[j]))

                def b4_load(j):
                    slx = j % 2
                    sv, vv = xg.get(f"x{slx}", (pV, 0))
                    if vv:
                        SY.wait_ge(sv.h, vv)
                    ld = SY.dma_start(x_t[:, slx * D:(slx + 1) * D],
                                      x_d[b * t_len + j * 128:
                                          b * t_len + j * 128 + 128, :])
                    dxs[slx].inc(ld, 16)
                    xv[j] = dxs[slx].n

                def b4_G(j):
                    sl = j % 3
                    slx = j % 2
                    x1 = x1s[j % 4]
                    cb, cbt = cb2(j), cb3(j)
                    vr2 = ln_rstd(cb, ln_mid(cb, vs2[j]))
                    rbc = stat[:, cb + 5:cb + 6].rearrange(
                        "p (a o) -> p a o", o=1).broadcast_to((128, 1, D))
                    V(lambda x1=x1, sl=sl, cb=cb, cbt=cbt, rbc=rbc:
                      VE.scalar_tensor_tensor(
                        x1[:, :].rearrange("p (a o) -> p a o", a=1),
                        ats[sl][:, :].rearrange("p (a o) -> p a o", a=1),
                        stat[:, cb + 2:cb + 3], rbc,
                        op0=ALU.subtract, op1=ALU.mult,
                        accum_out=stat[:, cbt + 6:cbt + 7]),
                      (pS, vr2), (dxs[slx], xv[j]))
                    vv = V(lambda x1=x1, slx=slx: VE.tensor_add(
                        x1[:, :], x1[:, :], x_t[:, slx * D:(slx + 1) * D]))
                    xg[f"x{slx}"] = (pV, vv)
                    V(lambda cbt=cbt, j=j: VE.tensor_add(
                        stat[:, cbt:cbt + 1], stat[:, cbt + 6:cbt + 7],
                        sums_x[:, j:j + 1]))
                    vs3[j] = S(lambda x1=x1, cbt=cbt: SE.activation(
                        junk[:, :], x1[:, :], AF.Square, bias=0.0, scale=1.0,
                        accum_out=stat[:, cbt + 1:cbt + 2]), (pV, vv))

                def b4_H(j):
                    sl = j % 2
                    x1 = x1s[j % 4]
                    cbt = cb3(j)
                    vr3 = ln_rstd(cbt, ln_mid(cbt, vs3[j]))
                    v_n2 = ln_apply(x1[:, :], n1_t[:, sl * D:(sl + 1) * D],
                                    cbt, vr3, (pT, g_tpN.get(j - 2, 0)))
                    vv = g_evN.get(j - 1, 0)
                    if vv:
                        TE.wait_ge(pV.h, vv)
                    TW((pV, v_n2))
                    for dt in range(DT):
                        tp = TE.transpose(
                            tb[:, 1024 + dt * 128:1024 + dt * 128 + 128],
                            n1_t[:, sl * D + dt * 128:sl * D + dt * 128 + 128],
                            id_bf[:, :])
                    pT.inc(tp)
                    g_tpN[j] = pT.n
                    g_evN[j] = V(lambda sl=sl: VE.tensor_copy(
                        n2s[sl][:, :], tb[:, 1024:2048]), (pT, pT.n),
                        (pT, g_yT.get(j - 2, 0)))
                    for oc in range(2):
                        def mk(ps, oc=oc, sl=sl):
                            for dt in range(DT):
                                TE.matmul(
                                    ps,
                                    n2s[sl][:, dt * 128:dt * 128 + 128],
                                    m_ld[:, dt * D + oc * 512:
                                         dt * D + oc * 512 + 512],
                                    start=(dt == 0), stop=False)
                            return TE.matmul(ps, on64[0:1, :],
                                             bias_bf[0:1, oc * 512:
                                                     oc * 512 + 512],
                                             start=False, stop=True)

                        def ev(ps, tv, oc=oc, sl=sl, j=j):
                            c4 = cb4(j)
                            vv = S(lambda: SE.activation(
                                yts[sl][:, oc * 512:oc * 512 + 512], ps,
                                AF.Identity, bias=0.0, scale=1.0,
                                accum_out=stat[:, c4 + 6 + oc:c4 + 7 + oc]),
                                (pT, tv))
                            return (pS, vv)
                        w0 = (pV, g_evN[j])
                        w1 = (dwm, v_mld) if (j == 0 and oc == 0) else (pV, 0)
                        yv = pj_chain(mk, ev, w0, w1)
                    g_yT[j] = pT.n
                    vs4[j] = S(lambda sl=sl, j=j: SE.activation(
                        junk[:, :], yts[sl][:, :], AF.Square, bias=0.0,
                        scale=1.0, accum_out=stat[:, cb4(j) + 1:
                                                  cb4(j) + 2]), (pS, yv))

                def b4_I(j):
                    sl = j % 2
                    x1 = x1s[j % 4]
                    c4 = cb4(j)
                    V(lambda c4=c4: VE.tensor_add(
                        stat[:, c4:c4 + 1], stat[:, c4 + 6:c4 + 7],
                        stat[:, c4 + 7:c4 + 8]), (pS, vs4[j]))
                    vr4 = ln_rstd(c4, ln_mid(c4, vs4[j]))
                    sv, vv = xg.get(f"x2{sl}", (dos[sl], 0))
                    x2a = x2_t[:, sl * D:(sl + 1) * D]
                    ln_apply(yts[sl][:, :], x2a, c4, vr4,
                             (sv, vv) if vv else (pV, 0))
                    v_x2 = V(lambda x2a=x2a, x1=x1: VE.tensor_add(
                        x2a, x2a, x1[:, :]))
                    SY.wait_ge(pV.h, v_x2)
                    st = SY.dma_start(out_d[b * t_len + j * 128:
                                            b * t_len + j * 128 + 128, :],
                                      x2a)
                    dos[sl].inc(st, 16)
                    xg[f"x2{sl}"] = (dos[sl], dos[sl].n)

                for r in range(TT + 7):
                    if r < TT:
                        b4_F(r)
                    if 0 <= r - 1 < TT:
                        b4_load(r - 1)
                    if 0 <= r - 2 < TT:
                        b4_G(r - 2)
                    if 0 <= r - 4 < TT:
                        b4_H(r - 4)
                    if 0 <= r - 5 < TT:
                        b4_I(r - 5)
                xg["psum_b67"] = (pV, pV.n)
                xg["n1T_w"] = (pT, pT.n)
                xg["slotA"] = (pV, pV.n)
                xg["slotA2"] = (pT, pT.n)
                xg["pt_w"] = (pV, pV.n)

        SY.wait_ge(dos[0].h, dos[0].n)
        SY.wait_ge(dos[1].h, dos[1].n)
        SY.wait_ge(dxs[0].h, dxs[0].n)
        SY.wait_ge(dxs[1].h, dxs[1].n)
        SY.wait_ge(dtp.h, dtp.n)
        SY.wait_ge(dms.h, dms.n)
        SY.wait_ge(dsm.h, dsm.n)
        GP.wait_ge(dwA.h, dwA.n)
        GP.wait_ge(dwB.h, dwB.n)
        GP.wait_ge(dwc.h, dwc.n)
        GP.wait_ge(dwm.h, dwm.n)
        GP.wait_ge(dd.h, dd.n)

    nc.finalize()
    return nc


_HOST = {}


def _host_consts():
    if "cc" not in _HOST:
        k = np.arange(D, dtype=np.int64)
        ang = (2.0 * np.pi / D) * ((np.outer(k, k) % D).astype(np.float64))
        _HOST["cc"] = np.ascontiguousarray(np.cos(ang).astype(np.float32))
        _HOST["ss"] = np.ascontiguousarray(np.sin(ang).astype(np.float32))
        _HOST["id"] = np.eye(128, dtype=np.float32)
        e65 = np.zeros((128, 65), dtype=np.float32)
        e65[:64, 0] = 1.0
        e65[64:, 64] = 1.0
        _HOST["e65"] = e65
        _HOST["on1"] = np.ones((1, 128), dtype=np.float32)
    return _HOST


def kernel(L0_attn_x, L0_attn_Wq, L0_attn_Wk, L0_attn_Wv,
           L0_ffn_w_freq_real, L0_ffn_w_freq_imag, L0_ffn_bias):
    x = np.ascontiguousarray(np.asarray(L0_attn_x, dtype=np.float32))
    B, T, _ = x.shape
    b_loc = B // NCORES
    hc = _host_consts()
    base = dict(
        wqT=np.ascontiguousarray(np.asarray(L0_attn_Wq, np.float32).T),
        wkT=np.ascontiguousarray(np.asarray(L0_attn_Wk, np.float32).T),
        wvT=np.ascontiguousarray(np.asarray(L0_attn_Wv, np.float32).T),
        wr=np.ascontiguousarray(np.asarray(L0_ffn_w_freq_real, np.float32)),
        wi=np.ascontiguousarray(np.asarray(L0_ffn_w_freq_imag, np.float32)),
        bias=np.ascontiguousarray(np.asarray(L0_ffn_bias, np.float32)
                                  .reshape(1, D)),
        cc=hc["cc"], ssn=hc["ss"], idm=hc["id"], e65=hc["e65"], on1=hc["on1"],
    )
    nc = build_nc(b_loc=b_loc, t_len=T)
    in_maps = []
    for c in range(NCORES):
        m = dict(base)
        m["x"] = np.ascontiguousarray(
            x[c * b_loc:(c + 1) * b_loc].reshape(b_loc * T, D))
        in_maps.append(m)
    res = run_bass_kernel_spmd(nc, in_maps, core_ids=list(range(NCORES)))
    outs = [np.asarray(r["out"]).reshape(b_loc, T, D) for r in res.results]
    return np.concatenate(outs, axis=0)


# version 16
# speedup vs baseline: 1.2843x; 1.0005x over previous
import numpy as np

import concourse.bass as bass
import concourse.mybir as mybir
from concourse.bass_utils import run_bass_kernel_spmd

F32 = mybir.dt.float32
BF16 = mybir.dt.bfloat16
AF = mybir.ActivationFunctionType
ALU = mybir.AluOpType
AX = mybir.AxisListType

D = 1024
H = 16
DT = 8
NCORES = 8
EPS = 1e-5
ISCALE = 1.0 / 32.0


class Sem:
    def __init__(self, nc, name):
        self.h = nc.semaphore(name).__enter__()
        self.n = 0

    def inc(self, instr, k=1):
        instr.then_inc(self.h, k)
        self.n += k
        return self.n


def chunks(total, size):
    out, off = [], 0
    while off < total:
        out.append((off, min(size, total - off)))
        off += size
    return out


def build_nc(b_loc=2, t_len=2048):
    TT = t_len // 128
    KT = t_len // 128
    QH = min(1024, t_len)
    NH = t_len // QH

    nc = bass.Bass("TRN2", target_bir_lowering=False)

    x_d = nc.dram_tensor("x", [b_loc * t_len, D], F32, kind="ExternalInput")
    wqT_d = nc.dram_tensor("wqT", [D, D], F32, kind="ExternalInput")
    wkT_d = nc.dram_tensor("wkT", [D, D], F32, kind="ExternalInput")
    wvT_d = nc.dram_tensor("wvT", [D, D], F32, kind="ExternalInput")
    wr_d = nc.dram_tensor("wr", [D, D], F32, kind="ExternalInput")
    wi_d = nc.dram_tensor("wi", [D, D], F32, kind="ExternalInput")
    bias_d = nc.dram_tensor("bias", [1, D], F32, kind="ExternalInput")
    cc_d = nc.dram_tensor("cc", [D, D], F32, kind="ExternalInput")
    ssn_d = nc.dram_tensor("ssn", [D, D], F32, kind="ExternalInput")
    id_d = nc.dram_tensor("idm", [128, 128], F32, kind="ExternalInput")
    e65_d = nc.dram_tensor("e65", [128, 65], F32, kind="ExternalInput")
    on1_d = nc.dram_tensor("on1", [1, 128], F32, kind="ExternalInput")
    out_d = nc.dram_tensor("out", [b_loc * t_len, D], F32, kind="ExternalOutput")
    dnm_d = nc.dram_tensor("dnm", [H, t_len], BF16, kind="Internal")
    m_d = nc.dram_tensor("m_hbm", [128, DT * D], BF16, kind="Internal")

    def wtiles(d):
        return d[:, :].rearrange("(t p) f -> p t f", p=128)

    ARENA = 202 * 1024
    _arena = nc.alloc_sbuf_tensor("arena", [128, ARENA], mybir.dt.uint8)
    _ab = nc.lookup_mloc(_arena).addr
    base = [_ab]
    amap = {}

    def A(name, shape, dtype, at=None):
        size = int(np.prod(shape[1:])) * (2 if dtype == BF16 else 4)
        size = (size + 31) // 32 * 32
        off = base[0] if at is None else at
        if at is None:
            base[0] += size
        amap[name] = off
        return nc.alloc_sbuf_tensor_at(name, list(shape), dtype, offset=off)

    r1_pad = max(DT * t_len * 2, 2 * DT * D * 2)
    n1T = A("n1T", [128, DT * t_len], BF16)
    if r1_pad > DT * t_len * 2:
        base[0] += r1_pad - DT * t_len * 2
    attnT = A("attnT", [128, DT * t_len], BF16)
    if r1_pad > DT * t_len * 2:
        base[0] += r1_pad - DT * t_len * 2
    vaug = A("vaug", [128, KT * (H * 65)], BF16)
    w_bf = A("w_bf", [128, 2 * DT * D], BF16)
    qT_b = A("qT_b", [128, 2 * t_len], BF16)
    kT_b = A("kT_b", [128, 2 * t_len], BF16)
    pt_sb = A("pt_sb", [128, 2 * QH], BF16)
    if 2 * QH * 2 < 4096:
        base[0] += 4096 - 2 * QH * 2
    qsq = A("qsq", [128, t_len], BF16)
    if t_len * 2 < 4096:
        base[0] += 4096 - t_len * 2
    x_t = A("x_t", [128, 2 * D], F32)
    id_bf = A("id_bf", [128, 128], BF16)
    n1_t = A("n1_t", [128, 2 * D], BF16)
    a_tok = A("a_tok", [128, D], F32)
    tmp_t = A("tmp_t", [128, D], F32)
    x1_t = A("x1_t", [128, D], F32)
    x2_t = A("x2_t", [128, 2 * D], F32)
    n2T_t = A("n2T_t", [128, D], BF16)
    m_stage = A("m_stage", [128, 512], BF16)
    recl = A("recl", [128, TT * 16], BF16)
    recl_f = A("recl_f", [128, TT * 16], F32)
    rex = A("rex", [128, D], F32)
    e65_bf = A("e65_bf", [128, 65], BF16)
    on1_f = A("on1_f", [1, 128], F32)
    on64 = A("on64", [65, 128], BF16)
    negb = A("negb", [128, H], F32)
    nb_bf = A("nb_bf", [65, 1], BF16)
    stat = A("stat", [128, 80], F32)
    st2 = A("st2", [65, 12], F32)
    eps_t = A("eps_t", [128, 1], F32)
    fence_t = A("fence_t", [128, 1], F32)
    sums_x = A("sums_x", [128, 16], F32)
    m_ld = nc.alloc_sbuf_tensor_at("m_ld", [128, DT * D], BF16,
                                   offset=amap["n1T"])
    p1T = nc.alloc_sbuf_tensor_at("p1T", [128, DT * D], BF16,
                                  offset=amap["n1T"])
    p2T = nc.alloc_sbuf_tensor_at("p2T", [128, DT * D], BF16,
                                  offset=amap["n1T"] + DT * D * 2)
    cc_bf = nc.alloc_sbuf_tensor_at("cc_bf", [128, DT * D], BF16,
                                    offset=amap["attnT"])
    ss_bf = nc.alloc_sbuf_tensor_at("ss_bf", [128, DT * D], BF16,
                                    offset=amap["attnT"] + DT * D * 2)
    stage = nc.alloc_sbuf_tensor_at("stage", [1, t_len], F32,
                                    offset=amap["x_t"])
    bias_s = nc.alloc_sbuf_tensor_at("bias_s", [1, D], F32, offset=amap["rex"])
    x1b1 = nc.alloc_sbuf_tensor_at("x1b1", [128, D], F32, offset=amap["rex"])
    x1b2 = nc.alloc_sbuf_tensor_at("x1b2", [128, D], F32, offset=amap["w_bf"])
    n2Tb = nc.alloc_sbuf_tensor_at("n2Tb", [128, D], BF16,
                                   offset=amap["w_bf"] + 4096)
    junk = nc.alloc_sbuf_tensor_at("junk", [128, D], BF16,
                                   offset=amap["w_bf"] + 6144)
    x1b4 = nc.alloc_sbuf_tensor_at("x1b4", [128, D], F32,
                                   offset=amap["w_bf"] + 8192)
    atok3 = nc.alloc_sbuf_tensor_at("atok3", [128, D], F32,
                                    offset=amap["w_bf"] + 12288)
    y_t0 = nc.alloc_sbuf_tensor_at("y_t0", [128, D], F32, offset=amap["pt_sb"])
    y_t1 = nc.alloc_sbuf_tensor_at("y_t1", [128, D], F32, offset=amap["qsq"])
    bias_bf = A("bias_bf", [1, D], BF16)
    assert base[0] <= _ab + ARENA, f"SBUF over: {base[0]-_ab}"

    pT = Sem(nc, "pT")
    pV = Sem(nc, "pV")
    pS = Sem(nc, "pS")
    dxs = [Sem(nc, "dx0"), Sem(nc, "dx1")]
    dsm = Sem(nc, "dsm")
    dwA = Sem(nc, "dwA")
    dwB = Sem(nc, "dwB")
    dwc = Sem(nc, "dwc")
    dwm = Sem(nc, "dwm")
    dms = Sem(nc, "dms")
    dd = Sem(nc, "dd")
    dtp = Sem(nc, "dtp")
    dos = [Sem(nc, "do0"), Sem(nc, "do1")]

    TE, VE, SE, SY, GP = nc.tensor, nc.vector, nc.scalar, nc.sync, nc.gpsimd

    def V(fn, *waits):
        for (s, v) in waits:
            if v:
                VE.wait_ge(s.h, v)
        VE.wait_ge(pV.h, pV.n)
        return pV.inc(fn())

    def S(fn, *waits):
        for (s, v) in waits:
            if v:
                SE.wait_ge(s.h, v)
        SE.wait_ge(pS.h, pS.n)
        return pS.inc(fn())

    def TW(*waits):
        for (s, v) in waits:
            if v:
                TE.wait_ge(s.h, v)

    xg = {}

    def g_wait_T(key):
        s, v = xg.get(key, (pT, 0))
        if v:
            TE.wait_ge(s.h, v)

    def ln_begin(src_ap, cb, sum_on="S", src_w=None):
        w = [src_w] if src_w else []
        if sum_on == "V":
            V(lambda: VE.reduce_sum(stat[:, cb:cb + 1], src_ap, axis=AX.X),
              *w)
        else:
            S(lambda: SE.activation(junk[:, :], src_ap, AF.Identity, bias=0.0,
                                    scale=1.0, accum_out=stat[:, cb:cb + 1]),
              *w)
        return S(lambda: SE.activation(junk[:, :], src_ap, AF.Square,
                                       bias=0.0, scale=1.0,
                                       accum_out=stat[:, cb + 1:cb + 2]), *w)

    def ln_mid(cb, vb, sum_ap=None):
        sap = stat[:, cb:cb + 1] if sum_ap is None else sum_ap
        V(lambda: VE.tensor_scalar_mul(stat[:, cb + 2:cb + 3],
                                       sap, 1.0 / D), (pS, vb))
        V(lambda: VE.tensor_mul(stat[:, cb + 3:cb + 4], stat[:, cb + 2:cb + 3],
                                stat[:, cb + 2:cb + 3]))
        return V(lambda: VE.tensor_scalar(stat[:, cb + 4:cb + 5],
                                          stat[:, cb + 1:cb + 2], 1.0 / D,
                                          stat[:, cb + 3:cb + 4],
                                          op0=ALU.mult, op1=ALU.subtract))

    def ln_rstd(cb, vmid):
        S(lambda: SE.activation(stat[:, cb + 5:cb + 6], stat[:, cb + 4:cb + 5],
                                AF.Ln, bias=eps_t[:, :], scale=1.0),
          (pV, vmid))
        return S(lambda: SE.activation(stat[:, cb + 5:cb + 6],
                                       stat[:, cb + 5:cb + 6], AF.Exp,
                                       bias=0.0, scale=-0.5))

    def ln_stats(src_ap, cb, mode="VS", src_w=None):
        vb = ln_begin(src_ap, cb, sum_on=mode[0], src_w=src_w)
        return ln_rstd(cb, ln_mid(cb, vb))

    def ln_apply(src_ap, out_ap, cb, vs, *waits):
        return V(lambda: VE.tensor_scalar(out_ap, src_ap,
                                          stat[:, cb + 2:cb + 3],
                                          stat[:, cb + 5:cb + 6],
                                          op0=ALU.subtract, op1=ALU.mult),
                 (pS, vs), *waits)

    def ln_block(src_ap, out_ap, residual_ap):
        V(lambda: VE.reduce_sum(stat[:, 0:1], src_ap, axis=AX.X))
        V(lambda: VE.scalar_tensor_tensor(tmp_t[:, :], src_ap, 1.0, src_ap,
                                          op0=ALU.mult, op1=ALU.mult,
                                          accum_out=stat[:, 1:2]))
        V(lambda: VE.tensor_scalar_mul(stat[:, 2:3], stat[:, 0:1], 1.0 / D))
        V(lambda: VE.tensor_mul(stat[:, 3:4], stat[:, 2:3], stat[:, 2:3]))
        vv = V(lambda: VE.tensor_scalar(stat[:, 4:5], stat[:, 1:2], 1.0 / D,
                                        stat[:, 3:4], op0=ALU.mult,
                                        op1=ALU.subtract))
        S(lambda: SE.activation(stat[:, 5:6], stat[:, 4:5], AF.Ln,
                                bias=eps_t[:, :], scale=1.0), (pV, vv))
        vs = S(lambda: SE.activation(stat[:, 5:6], stat[:, 5:6], AF.Exp,
                                     bias=0.0, scale=-0.5))
        if residual_ap is None:
            return V(lambda: VE.tensor_scalar(out_ap, src_ap, stat[:, 2:3],
                                              stat[:, 5:6], op0=ALU.subtract,
                                              op1=ALU.mult), (pS, vs))
        V(lambda: VE.tensor_scalar(tmp_t[:, :], src_ap, stat[:, 2:3],
                                   stat[:, 5:6], op0=ALU.subtract,
                                   op1=ALU.mult), (pS, vs))
        return V(lambda: VE.tensor_add(out_ap, tmp_t[:, :], residual_ap))

    with (
        nc.psum_tensor("pv_ps", [128, 1024], F32) as pv_ps,
        nc.psum_tensor("pj_ps", [128, 1024], F32) as pj_ps,
    ):
        pjg = [(pV, 0), (pV, 0)]
        pj_par = [0]

        def pj_chain(emit_mms, evac, *mm_waits):
            w = pj_par[0]
            s, v = pjg[w]
            if v:
                TE.wait_ge(s.h, v)
            TW(*mm_waits)
            last = emit_mms(pj_ps[:, w * 512:(w + 1) * 512])
            pT.inc(last)
            g = evac(pj_ps[:, w * 512:(w + 1) * 512], pT.n)
            if not isinstance(g, tuple):
                g = (pV, g)
            pjg[w] = g
            pj_par[0] ^= 1
            return g[1]

        dwc.inc(GP.dma_start(id_bf[:, :], id_d[:, :]), 16)
        dwc.inc(GP.dma_start(e65_bf[:, :], e65_d[:, :]), 16)
        dwc.inc(GP.dma_start(on64[0:1, :], on1_d[:, :]), 16)
        dwc.inc(GP.dma_start(on64[64:65, :], on1_d[:, :]), 16)
        for t in range(DT):
            dwc.inc(GP.dma_start(cc_bf[:, t * D:(t + 1) * D],
                                 wtiles(cc_d)[:, t, :]), 16)
        for t in range(DT):
            dwc.inc(GP.dma_start(ss_bf[:, t * D:(t + 1) * D],
                                 wtiles(ssn_d)[:, t, :]), 16)
        for t in range(DT):
            dwA.inc(GP.dma_start(w_bf[:, t * D:(t + 1) * D],
                                 wtiles(wr_d)[:, t, :]), 16)
        for t in range(DT):
            dwB.inc(GP.dma_start(w_bf[:, (DT + t) * D:(DT + t + 1) * D],
                                 wtiles(wi_d)[:, t, :]), 16)
        v_setup_c, v_setup_a, v_setup_b = dwc.n, dwA.n, dwB.n
        dsm.inc(SY.dma_start(bias_s[:, :], bias_d[:, :]), 16)
        dsm.inc(SY.dma_start(on1_f[:, :], on1_d[:, :]), 16)
        v_small = dsm.n

        V(lambda: VE.memset(eps_t[:, :], EPS))
        va = vaug[:, :].rearrange("p (k c) -> p k c", c=65)
        V(lambda: VE.memset(va[:, :, 64:65], 1.0))

        V(lambda: VE.tensor_copy(bias_bf[:, :], bias_s[:, :]),
          (dsm, v_small))

        TW((dwc, v_setup_c), (dwA, v_setup_a), (dwB, v_setup_b))

        def emit_mat(dst, pairs, spill=False):
            for i in range(DT):
                for oc in range(2):
                    def mk(ps, i=i, oc=oc):
                        first = True
                        np_ = len(pairs)
                        for pi, (lw, lblk, rt) in enumerate(pairs):
                            for dt in range(DT):
                                mm = TE.matmul(
                                    ps,
                                    lw[:, lblk + dt * D + i * 128:
                                       lblk + dt * D + i * 128 + 128],
                                    rt[:, dt * D + oc * 512:
                                       dt * D + oc * 512 + 512],
                                    start=first,
                                    stop=(pi == np_ - 1 and dt == DT - 1))
                                first = False
                        return mm

                    if not spill:
                        def ev(ps, tv, i=i, oc=oc):
                            return V(lambda: VE.tensor_copy(
                                dst[:, i * D + oc * 512:i * D + oc * 512 + 512],
                                ps), (pT, tv))
                    else:
                        def ev(ps, tv, i=i, oc=oc):
                            vv = V(lambda: VE.tensor_scalar_mul(
                                m_stage[:, :], ps, 1.0 / D), (pT, tv),
                                xg.get("m_stage", (pV, 0)))
                            SY.wait_ge(pV.h, vv)
                            st = SY.dma_start(
                                m_d[:, (i * 2 + oc) * 512:
                                    (i * 2 + oc + 1) * 512], m_stage[:, :])
                            dms.inc(st, 16)
                            xg["m_stage"] = (dms, dms.n)
                            return vv
                    pj_chain(mk, ev)

        emit_mat(p1T, [(w_bf, 0, cc_bf), (w_bf, DT * D, ss_bf)])
        v_sneg = V(lambda: VE.tensor_scalar_mul(ss_bf[:, :], ss_bf[:, :], -1.0),
                   (pT, pT.n))
        TW((pV, v_sneg))
        emit_mat(p2T, [(w_bf, DT * D, cc_bf), (w_bf, 0, ss_bf)])
        TW((pV, pV.n))
        emit_mat(None, [(p1T, 0, cc_bf), (p2T, 0, ss_bf)], spill=True)
        v_m_T = pT.n
        v_m_spill = dms.n
        xg["n1T_w"] = (pT, v_m_T)
        xg["attnT_w"] = (pT, v_m_T)
        xg["slotA"] = (pT, v_m_T)
        xg["slotA2"] = (pT, v_m_T)

        GP.wait_ge(pT.h, v_m_T)
        for t in range(DT):
            dwB.inc(GP.dma_start(w_bf[:, (DT + t) * D:(DT + t + 1) * D],
                                 wtiles(wkT_d)[:, t, :]), 16)
        v_wk = dwB.n

        for b in range(b_loc):
            with nc.psum_tensor(f"tb1_{b}", [128, 2048], BF16) as tb:
                g_wait_T("psum_b67")
                V(lambda: VE.memset(fence_t[0:1, 0:1], 0.0), (pT, pT.n))
                tbg = [(pV, 0), (pV, 0)]
                first_evac = [True]
                vs1 = {}
                xv1 = {}
                g_tp1 = {}

                def b1_F(j):
                    sl = j % 2
                    sv, vv = xg.get(f"x{sl}", (pV, 0))
                    if vv:
                        SY.wait_ge(sv.h, vv)
                    sv, vv = xg.get("stage_rd", (dd, 0))
                    if vv and j < 2:
                        SY.wait_ge(sv.h, vv)
                    ld = SY.dma_start(x_t[:, sl * D:(sl + 1) * D],
                                      x_d[b * t_len + j * 128:
                                          b * t_len + j * 128 + 128, :])
                    dxs[sl].inc(ld, 16)
                    xv1[j] = dxs[sl].n
                    V(lambda sl=sl, j=j: VE.reduce_sum(
                        sums_x[:, j:j + 1], x_t[:, sl * D:(sl + 1) * D],
                        axis=AX.X), (dxs[sl], xv1[j]))
                    vs1[j] = S(lambda sl=sl: SE.activation(
                        junk[:, :], x_t[:, sl * D:(sl + 1) * D], AF.Square,
                        bias=0.0, scale=1.0,
                        accum_out=stat[:, 8 * sl + 1:8 * sl + 2]),
                        (dxs[sl], xv1[j]))

                def b1_G(j):
                    sl = j % 2
                    vr1 = ln_rstd(8 * sl, ln_mid(8 * sl, vs1[j],
                                                 sums_x[:, j:j + 1]))
                    v_ap = ln_apply(x_t[:, sl * D:(sl + 1) * D],
                                    n1_t[:, sl * D:(sl + 1) * D], 8 * sl,
                                    vr1, (pT, g_tp1.get(j - 2, 0)))
                    xg[f"x{sl}"] = (pV, v_ap)
                    sv, vv = tbg[sl]
                    if vv:
                        TE.wait_ge(sv.h, vv)
                    TW((pV, v_ap))
                    for dt in range(DT):
                        tp = TE.transpose(
                            tb[:, sl * 1024 + dt * 128:sl * 1024 + dt * 128 + 128],
                            n1_t[:, sl * D + dt * 128:sl * D + dt * 128 + 128],
                            id_bf[:, :])
                    pT.inc(tp)
                    g_tp1[j] = pT.n
                    dst = n1T[:, :].rearrange("p (t q) -> p t q", q=t_len)[
                        :, :, j * 128:j * 128 + 128]
                    src = tb[:, sl * 1024:sl * 1024 + 1024].rearrange(
                        "p (t q) -> p t q", q=128)
                    ex = (xg["n1T_w"],) if first_evac[0] else ()
                    first_evac[0] = False
                    vv = V(lambda dst=dst, src=src: VE.tensor_copy(dst, src),
                           (pT, pT.n), *ex)
                    tbg[sl] = (pV, vv)

                for r in range(TT + 1):
                    if r < TT:
                        b1_F(r)
                    if r - 1 >= 0:
                        b1_G(r - 1)
                v_b1 = pV.n
                xg["psum_b67"] = (pV, pV.n)
                xg["slotA"] = (pV, pV.n)

            with (
                nc.psum_tensor(f"s0_{b}", [128, 1024], F32) as s0_ps,
                nc.psum_tensor(f"s1_{b}", [128, 1024], F32) as s1_ps,
            ):
                g_wait_T("psum_b67")
                g = xg["slotA"]
                GP.wait_ge(g[0].h, g[1])
                g = xg.get("slotA2", (pT, 0))
                if g[1]:
                    GP.wait_ge(g[0].h, g[1])
                for t in range(DT):
                    dwA.inc(GP.dma_start(w_bf[:, t * D:(t + 1) * D],
                                         wtiles(wvT_d)[:, t, :]), 16)
                v_wv = dwA.n
                vaug_g = xg.get("vaug_w", (pT, 0))
                first_vevac = [True]
                for j in range(TT):
                    for oc in range(2):
                        def mk(ps, j=j, oc=oc):
                            for dt in range(DT):
                                mm = TE.matmul(
                                    ps,
                                    n1T[:, dt * t_len + j * 128:
                                        dt * t_len + j * 128 + 128],
                                    w_bf[:, dt * D + oc * 512:
                                         dt * D + oc * 512 + 512],
                                    start=(dt == 0), stop=(dt == DT - 1))
                            return mm

                        def ev(ps, tv, j=j, oc=oc):
                            dstv = vaug[:, j * (H * 65):
                                        (j + 1) * (H * 65)].rearrange(
                                "p (h c) -> p h c", c=65)[
                                :, oc * 8:(oc + 1) * 8, 0:64]
                            srcv = ps.rearrange("p (h c) -> p h c", c=64)
                            ex = (vaug_g,) if first_vevac[0] else ()
                            first_vevac[0] = False
                            return V(lambda: VE.tensor_copy(dstv, srcv),
                                     (pT, tv), *ex)
                        w0 = (dwA, v_wv) if (j == 0 and oc == 0) else (pV, 0)
                        w1 = (pV, v_b1) if (j == 0 and oc == 0) else (pV, 0)
                        pj_chain(mk, ev, w0, w1)
                GP.wait_ge(pT.h, pT.n)
                for t in range(DT):
                    dwA.inc(GP.dma_start(w_bf[:, t * D:(t + 1) * D],
                                         wtiles(wqT_d)[:, t, :]), 16)
                xg["v_wq"] = dwA.n

                def bg_pair(p):
                    par = p % 2

                    def mkproj_a(wbase, tc, ln):
                        def go():
                            w = pj_par[0]
                            sg_, vg_ = pjg[w]
                            if vg_:
                                TE.wait_ge(sg_.h, vg_)
                            w0 = (dwA, xg["v_wq"]) if wbase == 0 else \
                                (dwB, v_wk)
                            TW(w0)
                            for dt in range(DT // 2):
                                TE.matmul(
                                    pj_ps[:, w * 512:w * 512 + ln],
                                    w_bf[:, wbase + dt * D + p * 128:
                                         wbase + dt * D + p * 128 + 128],
                                    n1T[:, dt * t_len + tc:
                                        dt * t_len + tc + ln],
                                    start=(dt == 0), stop=False)
                        return go

                    def mkproj_b(wbase, tc, ln):
                        def go():
                            w = pj_par[0]
                            qkg = xg.get(f"qk{par}", (pT, 0))
                            for dt in range(DT // 2, DT):
                                mm = TE.matmul(
                                    pj_ps[:, w * 512:w * 512 + ln],
                                    w_bf[:, wbase + dt * D + p * 128:
                                         wbase + dt * D + p * 128 + 128],
                                    n1T[:, dt * t_len + tc:
                                        dt * t_len + tc + ln],
                                    start=False, stop=(dt == DT - 1))
                            pT.inc(mm)
                            dstT = qT_b if wbase == 0 else kT_b
                            vv = V(lambda: VE.tensor_copy(
                                dstT[:, par * t_len + tc:
                                     par * t_len + tc + ln],
                                pj_ps[:, w * 512:w * 512 + ln]),
                                (pT, pT.n), qkg)
                            pjg[w] = (pV, vv)
                            pj_par[0] ^= 1
                        return go

                    for wbase in (0, DT * D):
                        for tc, ln in chunks(t_len, 512):
                            yield (4, mkproj_a(wbase, tc, ln))
                            yield (4, mkproj_b(wbase, tc, ln))

                    yield (0, lambda: xg.__setitem__("v_qkev", pV.n))

                    def mk_qsq(srcT):
                        def go():
                            xg["v_qsq"] = V(
                                lambda: VE.scalar_tensor_tensor(
                                    qsq[:, :],
                                    srcT[:, par * t_len:(par + 1) * t_len],
                                    1.0,
                                    srcT[:, par * t_len:(par + 1) * t_len],
                                    op0=ALU.mult, op1=ALU.mult),
                                (pV, xg["v_qkev"]), xg.get("qsq_rd", (pT, 0)),
                                xg.get("pt_w", (pV, 0)))
                        return go

                    def mk_norm(tc, ln, col):
                        def go():
                            def mk(ps):
                                return TE.matmul(ps[0:65, 0:ln], e65_bf[:, :],
                                                 qsq[:, tc:tc + ln],
                                                 start=True, stop=True)

                            def ev(ps, tv):
                                vv = V(lambda: VE.reduce_max(
                                    st2[:, col:col + 1], ps[0:65, 0:ln],
                                    axis=AX.X), (pT, tv))
                                xg["qsq_rd"] = (pT, tv)
                                return vv
                            pj_chain(mk, ev, (pV, xg["v_qsq"]))
                        return go

                    yield (0, mk_qsq(qT_b))
                    for ci, (tc, ln) in enumerate(chunks(t_len, 512)):
                        yield (1, mk_norm(tc, ln, ci))
                    yield (0, mk_qsq(kT_b))
                    for ci, (tc, ln) in enumerate(chunks(t_len, 512)):
                        yield (1, mk_norm(tc, ln, 4 + ci))

                    def mk_fin():
                        def go():
                            nch = len(chunks(t_len, 512))
                            V(lambda: VE.tensor_reduce(
                                st2[:, 8:9], st2[:, 0:nch], axis=AX.X,
                                op=ALU.max))
                            V(lambda: VE.tensor_reduce(
                                st2[:, 9:10], st2[:, 4:4 + nch], axis=AX.X,
                                op=ALU.max))
                            vv = V(lambda: VE.tensor_mul(
                                st2[:, 10:11], st2[:, 8:9], st2[:, 9:10]))
                            S(lambda: SE.activation(
                                st2[:, 11:12], st2[:, 10:11], AF.Ln,
                                bias=eps_t[0:65, :], scale=1.0), (pV, vv))
                            vs = S(lambda: SE.activation(
                                st2[:, 11:12], st2[:, 11:12], AF.Exp,
                                bias=0.0, scale=0.5))
                            vn = V(lambda: VE.tensor_scalar_mul(
                                nb_bf[:, :], st2[:, 11:12], -1.0 / 32.0),
                                (pS, vs))

                            def mk(ps):
                                for h2 in range(2):
                                    mm = TE.matmul(
                                        ps[0:128, h2:h2 + 1],
                                        on64[64 * h2:64 * h2 + 1, :],
                                        nb_bf[64 * h2:64 * h2 + 1, :],
                                        start=True, stop=True)
                                return mm

                            def ev(ps, tv):
                                vv = V(lambda: VE.tensor_copy(
                                    negb[:, 2 * p:2 * p + 2],
                                    ps[0:128, 0:2]), (pT, tv))
                                xg[f"negb{p}"] = (pV, vv)
                                return vv
                            pj_chain(mk, ev, (pV, vn))
                        return go
                    yield (1, mk_fin())

                for _, go in bg_pair(0):
                    go()

                for p in range(H // 2):
                    par = p % 2
                    bg = list(bg_pair(p + 1)) if p + 1 < H // 2 else []
                    bgi = [0]
                    n_slots = 2 * NH * KT
                    bg_total = sum(c for c, _ in bg) or 1
                    bg_done = [0]

                    def bg_step():
                        while (bg and bgi[0] < len(bg)
                               and bg_done[0] / bg_total <
                               (slot[0] + 1) / n_slots):
                            c, go = bg[bgi[0]]
                            go()
                            bg_done[0] += c
                            bgi[0] += 1

                    v_qk_here = xg.get("v_qkev", pV.n)
                    slot = [0]
                    for h2 in range(2):
                        head = 2 * p + h2
                        hb = h2 * 64
                        ptg = xg.setdefault("ptg", [(pT, 0), (pT, 0)])
                        sg = [xg.get("psum_s0", (pS, 0)),
                              xg.get("psum_s1x", (pS, 0))]

                        def emit_S(kt, half, first):
                            sp = [s0_ps, s1_ps][kt % 2]
                            q0 = half * QH
                            s, v = sg[kt % 2]
                            if v:
                                TE.wait_ge(s.h, v)
                            if first:
                                TW((pV, v_qk_here))
                            for qc, ln in chunks(QH, 512):
                                smm = TE.matmul(
                                    sp[:, qc:qc + ln],
                                    kT_b[hb:hb + 64,
                                         par * t_len + kt * 128:
                                         par * t_len + kt * 128 + 128],
                                    qT_b[hb:hb + 64,
                                         par * t_len + q0 + qc:
                                         par * t_len + q0 + qc + ln],
                                    start=True, stop=True)
                            return pT.inc(smm)

                        for half in range(NH):
                            q0 = half * QH
                            v_smm_next = emit_S(0, half,
                                                half == 0 and h2 == 0)
                            for kt in range(KT):
                                sp = [s0_ps, s1_ps][kt % 2]
                                v_smm = v_smm_next
                                exw = [(pT, v_smm), ptg[kt % 2]]
                                if kt == 0 and half == 0:
                                    exw.append(xg.get(f"negb{p}", (pV, 0)))
                                    if h2 == 0 and p == 0:
                                        exw.append(xg.get("pt_w", (pV, 0)))
                                vexp = S(lambda kt=kt, sp=sp, head=head:
                                         SE.activation(
                                             pt_sb[:, (kt % 2) * QH:
                                                   (kt % 2) * QH + QH],
                                             sp[:, 0:QH], AF.Exp,
                                             bias=negb[:, head:head + 1],
                                             scale=ISCALE), *exw)
                                sg[kt % 2] = (pS, vexp)
                                if kt + 1 < KT:
                                    v_smm_next = emit_S(kt + 1, half, False)
                                bg_step()
                                slot[0] += 1
                                TW((pS, vexp))
                                if kt == 0:
                                    g_wait_T("psum_pv")
                                for qc, ln in chunks(QH, 512):
                                    pmm = TE.matmul(
                                        pv_ps[0:65, qc:qc + ln],
                                        vaug[:, kt * (H * 65) + head * 65:
                                             kt * (H * 65) + head * 65 + 65],
                                        pt_sb[:, (kt % 2) * QH + qc:
                                              (kt % 2) * QH + qc + ln],
                                        start=(kt == 0), stop=(kt == KT - 1))
                                pT.inc(pmm)
                                ptg[kt % 2] = (pT, pT.n)
                            dte = head // 2
                            aw = [(pT, pT.n)]
                            if b == 0 and head == 0 and half == 0:
                                aw.append(xg["attnT_w"])
                            S(lambda dte=dte, hb=hb, q0=q0: SE.activation(
                                attnT[hb:hb + 64,
                                      dte * t_len + q0:dte * t_len + q0 + QH],
                                pv_ps[0:64, 0:QH], AF.Identity, bias=0.0,
                                scale=1.0), *aw)
                            sw = [xg.get("stage_rd", (dd, 0))]
                            if head == 0 and half == 0:
                                sw += [xg.get("x0", (pV, 0)),
                                       xg.get("x1", (pV, 0))]
                            v_dn = V(lambda q0=q0: VE.tensor_copy(
                                stage[:, q0:q0 + QH], pv_ps[64:65, 0:QH]),
                                (pT, pT.n), *sw)
                            xg["psum_pv"] = (pV, v_dn)
                        xg["psum_s0"] = sg[0]
                        xg["psum_s1x"] = sg[1]
                        GP.wait_ge(pV.h, v_dn)
                        if head == 0:
                            s, v = xg.get("dnm_rd", (dtp, 0))
                            if v:
                                GP.wait_ge(s.h, v)
                        dn = GP.dma_start(dnm_d[head:head + 1, :], stage[:, :])
                        dd.inc(dn, 16)
                        xg["stage_rd"] = (dd, dd.n)
                    while bgi[0] < len(bg):
                        bg_step()
                    xg[f"qk{par}"] = (pT, pT.n)
                v_attn_T = pT.n
                v_attn_S = pS.n
                xg["vaug_w"] = (pT, pT.n)
                xg["psum_b67"] = (pV, pV.n)
                xg["psum_s1"] = xg.pop("psum_s1x")

            GP.wait_ge(pT.h, v_attn_T)
            GP.wait_ge(dms.h, v_m_spill)
            for i in range(4):
                dwm.inc(GP.dma_start(m_ld[:, i * 2048:(i + 1) * 2048],
                                     m_d[:, i * 2048:(i + 1) * 2048]), 16)
            v_mld = dwm.n
            SY.wait_ge(dd.h, dd.n)
            s, v = xg.get("recl_rd", (pV, 0))
            if v:
                SY.wait_ge(s.h, v)
            for j in range(TT):
                dtp.inc(SY.dma_start(recl[:, j * 16:(j + 1) * 16],
                                     dnm_d[0:16, j * 128:(j + 1) * 128],
                                     transpose=True), 16)
            xg["dnm_rd"] = (dtp, dtp.n)
            v_recl = V(lambda: VE.reciprocal(recl_f[:, :], recl[:, :]),
                       (dtp, dtp.n))
            xg["recl_rd"] = (pV, v_recl)

            with nc.psum_tensor(f"tb4_{b}", [128, 4096], BF16) as tb:
                g_wait_T("psum_b67")
                V(lambda: VE.memset(fence_t[0:1, 0:1], 0.0), (pT, v_attn_T))
                x1s = [x1_t, x1b1, x1b2, x1b4]
                ats = [a_tok, tmp_t, atok3]
                yts = [y_t0, y_t1]
                n2s = [n2T_t, n2Tb]
                vs2, vs3, vs4 = {}, {}, {}
                xv = {}
                g_div = {}
                g_tpN = {}
                g_yT = {}
                g_evN = {}

                def cb2(j):
                    return 8 * (j % 3)

                def cb3(j):
                    return 24 + 8 * (j % 3)

                def cb4(j):
                    return 48 + 8 * (j % 3)

                def b4_F(j):
                    sl = j % 3
                    tba = (j % 2) * 1024
                    vv = g_div.get(j - 2, 0)
                    if vv:
                        TE.wait_ge(pV.h, vv)
                    if j == 0:
                        TW((pS, v_attn_S))
                    for dt in range(DT):
                        tp = TE.transpose(
                            tb[:, tba + dt * 128:tba + dt * 128 + 128],
                            attnT[:, dt * t_len + j * 128:
                                  dt * t_len + j * 128 + 128],
                            id_bf[:, :])
                    pT.inc(tp)
                    bcsrc = recl_f[:, j * 16:(j + 1) * 16].rearrange(
                        "p (h o) -> p h o", o=1).broadcast_to((128, 16, 64))
                    cb = cb2(j)
                    g_div[j] = V(lambda bcsrc=bcsrc, sl=sl, cb=cb, tba=tba:
                                 VE.scalar_tensor_tensor(
                        ats[sl][:, :].rearrange("p (h o) -> p h o", h=16),
                        tb[:, tba:tba + 1024].rearrange(
                            "p (h o) -> p h o", h=16),
                        1.0, bcsrc, op0=ALU.mult, op1=ALU.mult,
                        accum_out=stat[:, cb:cb + 1]),
                        (pT, pT.n), (pV, v_recl))
                    vs2[j] = S(lambda sl=sl, cb=cb: SE.activation(
                        junk[:, :], ats[sl][:, :], AF.Square, bias=0.0,
                        scale=1.0, accum_out=stat[:, cb + 1:cb + 2]),
                        (pV, g_div[j]))

                def b4_load(j):
                    slx = j % 2
                    sv, vv = xg.get(f"x{slx}", (pV, 0))
                    if vv:
                        SY.wait_ge(sv.h, vv)
                    ld = SY.dma_start(x_t[:, slx * D:(slx + 1) * D],
                                      x_d[b * t_len + j * 128:
                                          b * t_len + j * 128 + 128, :])
                    dxs[slx].inc(ld, 16)
                    xv[j] = dxs[slx].n

                def b4_G(j):
                    sl = j % 3
                    slx = j % 2
                    x1 = x1s[j % 4]
                    cb, cbt = cb2(j), cb3(j)
                    vr2 = ln_rstd(cb, ln_mid(cb, vs2[j]))
                    rbc = stat[:, cb + 5:cb + 6].rearrange(
                        "p (a o) -> p a o", o=1).broadcast_to((128, 1, D))
                    V(lambda x1=x1, sl=sl, cb=cb, cbt=cbt, rbc=rbc:
                      VE.scalar_tensor_tensor(
                        x1[:, :].rearrange("p (a o) -> p a o", a=1),
                        ats[sl][:, :].rearrange("p (a o) -> p a o", a=1),
                        stat[:, cb + 2:cb + 3], rbc,
                        op0=ALU.subtract, op1=ALU.mult,
                        accum_out=stat[:, cbt + 6:cbt + 7]),
                      (pS, vr2), (dxs[slx], xv[j]))
                    vv = V(lambda x1=x1, slx=slx: VE.tensor_add(
                        x1[:, :], x1[:, :], x_t[:, slx * D:(slx + 1) * D]))
                    xg[f"x{slx}"] = (pV, vv)
                    V(lambda cbt=cbt, j=j: VE.tensor_add(
                        stat[:, cbt:cbt + 1], stat[:, cbt + 6:cbt + 7],
                        sums_x[:, j:j + 1]))
                    vs3[j] = S(lambda x1=x1, cbt=cbt: SE.activation(
                        junk[:, :], x1[:, :], AF.Square, bias=0.0, scale=1.0,
                        accum_out=stat[:, cbt + 1:cbt + 2]), (pV, vv))

                def b4_H(j):
                    sl = j % 2
                    x1 = x1s[j % 4]
                    cbt = cb3(j)
                    vr3 = ln_rstd(cbt, ln_mid(cbt, vs3[j]))
                    v_n2 = ln_apply(x1[:, :], n1_t[:, sl * D:(sl + 1) * D],
                                    cbt, vr3, (pT, g_tpN.get(j - 2, 0)))
                    vv = g_evN.get(j - 2, 0)
                    if vv:
                        TE.wait_ge(pV.h, vv)
                    TW((pV, v_n2))
                    tbn = 2048 + (j % 2) * 1024
                    for dt in range(DT):
                        tp = TE.transpose(
                            tb[:, tbn + dt * 128:tbn + dt * 128 + 128],
                            n1_t[:, sl * D + dt * 128:sl * D + dt * 128 + 128],
                            id_bf[:, :])
                    pT.inc(tp)
                    g_tpN[j] = pT.n
                    g_evN[j] = V(lambda sl=sl, tbn=tbn: VE.tensor_copy(
                        n2s[sl][:, :], tb[:, tbn:tbn + 1024]), (pT, pT.n),
                        (pT, g_yT.get(j - 2, 0)))
                    for oc in range(2):
                        def mk(ps, oc=oc, sl=sl):
                            for dt in range(DT):
                                TE.matmul(
                                    ps,
                                    n2s[sl][:, dt * 128:dt * 128 + 128],
                                    m_ld[:, dt * D + oc * 512:
                                         dt * D + oc * 512 + 512],
                                    start=(dt == 0), stop=False)
                            return TE.matmul(ps, on64[0:1, :],
                                             bias_bf[0:1, oc * 512:
                                                     oc * 512 + 512],
                                             start=False, stop=True)

                        def ev(ps, tv, oc=oc, sl=sl, j=j):
                            c4 = cb4(j)
                            vv = S(lambda: SE.activation(
                                yts[sl][:, oc * 512:oc * 512 + 512], ps,
                                AF.Identity, bias=0.0, scale=1.0,
                                accum_out=stat[:, c4 + 6 + oc:c4 + 7 + oc]),
                                (pT, tv))
                            return (pS, vv)
                        w0 = (pV, g_evN[j])
                        w1 = (dwm, v_mld) if (j == 0 and oc == 0) else (pV, 0)
                        yv = pj_chain(mk, ev, w0, w1)
                    g_yT[j] = pT.n
                    vs4[j] = S(lambda sl=sl, j=j: SE.activation(
                        junk[:, :], yts[sl][:, :], AF.Square, bias=0.0,
                        scale=1.0, accum_out=stat[:, cb4(j) + 1:
                                                  cb4(j) + 2]), (pS, yv))

                def b4_I(j):
                    sl = j % 2
                    x1 = x1s[j % 4]
                    c4 = cb4(j)
                    V(lambda c4=c4: VE.tensor_add(
                        stat[:, c4:c4 + 1], stat[:, c4 + 6:c4 + 7],
                        stat[:, c4 + 7:c4 + 8]), (pS, vs4[j]))
                    vr4 = ln_rstd(c4, ln_mid(c4, vs4[j]))
                    sv, vv = xg.get(f"x2{sl}", (dos[sl], 0))
                    x2a = x2_t[:, sl * D:(sl + 1) * D]
                    ln_apply(yts[sl][:, :], x2a, c4, vr4,
                             (sv, vv) if vv else (pV, 0))
                    v_x2 = V(lambda x2a=x2a, x1=x1: VE.tensor_add(
                        x2a, x2a, x1[:, :]))
                    SY.wait_ge(pV.h, v_x2)
                    st = SY.dma_start(out_d[b * t_len + j * 128:
                                            b * t_len + j * 128 + 128, :],
                                      x2a)
                    dos[sl].inc(st, 16)
                    xg[f"x2{sl}"] = (dos[sl], dos[sl].n)

                for r in range(TT + 7):
                    if 0 <= r - 5 < TT:
                        b4_I(r - 5)
                    if 0 <= r - 4 < TT:
                        b4_H(r - 4)
                    if 0 <= r - 2 < TT:
                        b4_G(r - 2)
                    if 0 <= r - 1 < TT:
                        b4_load(r - 1)
                    if r < TT:
                        b4_F(r)
                xg["psum_b67"] = (pV, pV.n)
                xg["n1T_w"] = (pT, pT.n)
                xg["slotA"] = (pV, pV.n)
                xg["slotA2"] = (pT, pT.n)
                xg["pt_w"] = (pV, pV.n)

        SY.wait_ge(dos[0].h, dos[0].n)
        SY.wait_ge(dos[1].h, dos[1].n)
        SY.wait_ge(dxs[0].h, dxs[0].n)
        SY.wait_ge(dxs[1].h, dxs[1].n)
        SY.wait_ge(dtp.h, dtp.n)
        SY.wait_ge(dms.h, dms.n)
        SY.wait_ge(dsm.h, dsm.n)
        GP.wait_ge(dwA.h, dwA.n)
        GP.wait_ge(dwB.h, dwB.n)
        GP.wait_ge(dwc.h, dwc.n)
        GP.wait_ge(dwm.h, dwm.n)
        GP.wait_ge(dd.h, dd.n)

    nc.finalize()
    return nc


_HOST = {}


def _host_consts():
    if "cc" not in _HOST:
        k = np.arange(D, dtype=np.int64)
        ang = (2.0 * np.pi / D) * ((np.outer(k, k) % D).astype(np.float64))
        _HOST["cc"] = np.ascontiguousarray(np.cos(ang).astype(np.float32))
        _HOST["ss"] = np.ascontiguousarray(np.sin(ang).astype(np.float32))
        _HOST["id"] = np.eye(128, dtype=np.float32)
        e65 = np.zeros((128, 65), dtype=np.float32)
        e65[:64, 0] = 1.0
        e65[64:, 64] = 1.0
        _HOST["e65"] = e65
        _HOST["on1"] = np.ones((1, 128), dtype=np.float32)
    return _HOST


def kernel(L0_attn_x, L0_attn_Wq, L0_attn_Wk, L0_attn_Wv,
           L0_ffn_w_freq_real, L0_ffn_w_freq_imag, L0_ffn_bias):
    x = np.ascontiguousarray(np.asarray(L0_attn_x, dtype=np.float32))
    B, T, _ = x.shape
    b_loc = B // NCORES
    hc = _host_consts()
    base = dict(
        wqT=np.ascontiguousarray(np.asarray(L0_attn_Wq, np.float32).T),
        wkT=np.ascontiguousarray(np.asarray(L0_attn_Wk, np.float32).T),
        wvT=np.ascontiguousarray(np.asarray(L0_attn_Wv, np.float32).T),
        wr=np.ascontiguousarray(np.asarray(L0_ffn_w_freq_real, np.float32)),
        wi=np.ascontiguousarray(np.asarray(L0_ffn_w_freq_imag, np.float32)),
        bias=np.ascontiguousarray(np.asarray(L0_ffn_bias, np.float32)
                                  .reshape(1, D)),
        cc=hc["cc"], ssn=hc["ss"], idm=hc["id"], e65=hc["e65"], on1=hc["on1"],
    )
    nc = build_nc(b_loc=b_loc, t_len=T)
    in_maps = []
    for c in range(NCORES):
        m = dict(base)
        m["x"] = np.ascontiguousarray(
            x[c * b_loc:(c + 1) * b_loc].reshape(b_loc * T, D))
        in_maps.append(m)
    res = run_bass_kernel_spmd(nc, in_maps, core_ids=list(range(NCORES)))
    outs = [np.asarray(r["out"]).reshape(b_loc, T, D) for r in res.results]
    return np.concatenate(outs, axis=0)
